# version 15
# baseline (speedup 1.0000x reference)
import numpy as np
import ml_dtypes
from contextlib import ExitStack

import concourse.bass as bass
import concourse.mybir as mybir
from concourse.bass_utils import run_bass_kernel_spmd

F32 = mybir.dt.float32
BF16 = mybir.dt.bfloat16
ALU = mybir.AluOpType
AF = mybir.ActivationFunctionType

B, T, C, H, D, I = 4, 2048, 2048, 16, 128, 8192
NF = C // 128
NT = T // 512
OWN = T // 2
NG = 8
NIP = I // 256
NIC = I // 128
EXP_BIAS = -2.0
EXP_SCALE = float(1.0 / np.sqrt(128.0))
LN_EPS = 1e-5

ENGINES = ("sync", "tensor", "scalar", "vector", "gpsimd")
DEBUG = False


class KB:

    def __init__(self, nc):
        self.nc = nc
        self.prog = {e: [] for e in ENGINES}
        self.model = {}
        self.watermark = {e: {} for e in ENGINES}
        self.sem_names = []

    def sem(self, name):
        if name not in self.model:
            self.sem_names.append(name)
            self.model[name] = 0
        return name

    def emit(self, eng, fn, waits=(), inc=None, inc_by=1):
        wl = []
        wm = self.watermark[eng]
        coal = {}
        for t in waits:
            if t is None:
                continue
            s, v = t
            coal[s] = max(coal.get(s, 0), v)
        for s, v in coal.items():
            if v > wm.get(s, 0):
                wm[s] = v
                wl.append((s, v))
        tok = None
        if inc is not None:
            self.sem(inc)
            self.model[inc] += inc_by
            tok = (inc, self.model[inc])
        self.prog[eng].append((tuple(wl), fn, (inc, inc_by) if inc else None))
        return tok

    def dma(self, eng, out, in_, sem, waits=()):
        return self.emit(eng, lambda e: e.dma_start(out=out, in_=in_),
                         waits, inc=sem, inc_by=16)

    def finalize(self, block, sems):
        for eng in ENGINES:
            ops = self.prog[eng]
            if not ops:
                continue

            def make(ops):
                def body(handle):
                    for waits, fn, inc in ops:
                        for s, v in waits:
                            handle.wait_ge(sems[s], v)
                        ins = fn(handle)
                        if inc is not None:
                            s, by = inc
                            ins.then_inc(sems[s], by)
                return body
            getattr(block, eng)(make(ops))


class Arena:

    def __init__(self, ap, cols):
        self.ap = ap
        self.cols = cols
        self.ptr = 0
        self.marks = []
        self.peak = 0

    def alloc(self, f32cols, dtype=F32):
        f32cols = int(f32cols)
        a = self.ptr
        assert a + f32cols <= self.cols, f"arena overflow {a}+{f32cols}>{self.cols}"
        self.ptr = a + f32cols
        self.peak = max(self.peak, self.ptr)
        v = self.ap[:, a:a + f32cols]
        if dtype == BF16:
            v = v.bitcast(BF16)
        return v

    def mark(self):
        self.marks.append(self.ptr)

    def release(self):
        self.ptr = self.marks.pop()


def build_nc():
    nc = bass.Bass("TRN2", target_bir_lowering=False)

    xT = nc.dram_tensor("xT", [C, T], F32, kind="ExternalInput").ap()
    wqT = nc.dram_tensor("wqT", [C, C], F32, kind="ExternalInput").ap()
    wkT = nc.dram_tensor("wkT", [C, C], F32, kind="ExternalInput").ap()
    wvT = nc.dram_tensor("wvT", [C, C], F32, kind="ExternalInput").ap()
    wpT = nc.dram_tensor("wpT", [C, C], F32, kind="ExternalInput").ap()
    wgu = nc.dram_tensor("wgu", [C, 2 * I], F32, kind="ExternalInput").ap()
    wd = nc.dram_tensor("wd", [I, C], F32, kind="ExternalInput").ap()
    ropeC = nc.dram_tensor("ropeC", [128, T], F32, kind="ExternalInput").ap()
    ropeS = nc.dram_tensor("ropeS", [128, T], F32, kind="ExternalInput").ap()
    pmat = nc.dram_tensor("pmat", [128, 128], BF16, kind="ExternalInput").ap()
    ident = nc.dram_tensor("ident", [128, 128], BF16, kind="ExternalInput").ap()
    masks = nc.dram_tensor("masks", [128, 9 * 512], BF16, kind="ExternalInput").ap()
    out = nc.dram_tensor("out", [C, OWN], F32, kind="ExternalOutput").ap()

    d_y = nc.dram_tensor("d_y", [C, OWN], BF16).ap()
    d_s = nc.dram_tensor("d_s", [C, OWN], F32).ap()
    d_hT = nc.dram_tensor("d_hT", [128, NIC * OWN], BF16).ap()
    d_wdb = nc.dram_tensor("d_wdb", [128, NIC * C], BF16).ap()
    dbg = {}
    if DEBUG:
        dbg["xn"] = nc.dram_tensor("dbg_xn", [128, NF * T], BF16).ap()
        dbg["k"] = nc.dram_tensor("dbg_k", [128, 2 * T], BF16).ap()
        dbg["q"] = nc.dram_tensor("dbg_q", [128, 2 * OWN], BF16).ap()
        dbg["v"] = nc.dram_tensor("dbg_v", [128, 16 * 256], BF16).ap()
        dbg["hsp"] = nc.dram_tensor("dbg_hsp", [128, 8 * 1024], BF16).ap()

    st = ExitStack()
    ARENA_COLS = 49088
    arena_t = st.enter_context(nc.sbuf_tensor("arena", [128, ARENA_COLS], F32))
    A = Arena(arena_t.ap(), ARENA_COLS)

    rcos = A.alloc(T // 2, BF16)
    rsin = A.alloc(T // 2, BF16)
    pm_sb = A.alloc(64, BF16)
    id_sb = A.alloc(64, BF16)
    mk_sb = A.alloc(9 * 256, BF16)
    ones_colb = A.alloc(1, BF16)
    ones_rowf = A.alloc(128)
    consts = A.alloc(2)
    A.mark()
    xn = A.alloc(NF * T // 2, BF16)

    nc.const_aps.register(F32, EXP_BIAS, consts[:, 0:1])
    nc.const_aps.register(F32, LN_EPS, consts[:, 1:2])

    ps = [st.enter_context(nc.psum_tensor(f"ps{i}", [128, 512], F32)).ap()
          for i in range(8)]

    kb = KB(nc)
    E = kb.emit
    for s in ["dRC", "dRS", "dPM", "dID", "dMK", "dX0", "dX1",
              "dW0", "dW1", "dW2", "dY", "dYB", "dWP0", "dWP1",
              "dG0", "dG1", "dG2", "dG3", "dS", "dHT0", "dHT1", "dHTB", "dHTB2", "dHTB3",
              "dWD0", "dWD1", "dWQ0", "dWQ1", "dWDS0", "dWDS1", "dWDB0", "dWDB1",
              "dSI0", "dSI1", "dXI0", "dXI1", "dOUT0", "dOUT1",
              "PE", "ACT", "DVE", "POOL",
              "dbg0", "dbg1", "dbg2", "dbg3", "dbg4"]:
        kb.sem(s)

    xnv = xn[:].rearrange("p (c x) -> p c x", x=T)

    A.mark()
    cst32 = A.alloc(2 * T)
    t_rc32 = kb.dma("sync", cst32[:, 0:T], ropeC[:], "dRC")
    t_rs32 = kb.dma("sync", cst32[:, T:2 * T], ropeS[:], "dRS")
    t_rcc = E("vector", lambda e: e.tensor_copy(rcos[:], cst32[:, 0:T]), [t_rc32], "DVE")
    t_rsc = E("vector", lambda e: e.tensor_copy(rsin[:], cst32[:, T:2 * T]), [t_rs32], "DVE")
    t_pm = kb.dma("sync", pm_sb[:], pmat[:], "dPM")
    t_id = kb.dma("sync", id_sb[:], ident[:], "dID")
    t_mk = kb.dma("sync", mk_sb[:], masks[:], "dMK")
    t_c0 = E("vector", lambda e: e.memset(consts[:, 0:1], EXP_BIAS), [], "DVE")
    t_c1 = E("vector", lambda e: e.memset(consts[:, 1:2], LN_EPS), [], "DVE")
    t_oc = E("vector", lambda e: e.memset(ones_colb[:, 0:1], 1.0), [], "DVE")
    t_or = E("vector", lambda e: e.memset(ones_rowf[0:1, :], 1.0), [], "DVE")
    A.release()

    A.mark()
    x_f32_2 = [A.alloc(NF * 512) for _ in range(2)]
    x_bf_1 = A.alloc(NF * 256, BF16)
    x_bf_2 = [x_bf_1, x_bf_1]
    xsq_bf = A.alloc(NF * 256, BF16)
    statm = A.alloc(512)
    statv = A.alloc(512)
    mbc = A.alloc(512)
    rbc = A.alloc(512)
    msq = A.alloc(512)

    xqv = xsq_bf[:].rearrange("p (c x) -> p c x", x=512)
    xn_done = []
    prev_reads = [t_rcc, t_rsc]
    prev_sub = {}
    prev_sq = None
    prev_subx = None
    prev_sqx = None
    prev_cb = {}
    prev_mbc = None
    prev_rbc = None
    for tt in range(NT):
        par = tt % 2
        x_f32 = x_f32_2[par]
        x_bf = x_bf_2[par]
        xbv = x_bf[:].rearrange("p (c x) -> p c x", x=512)
        xfv = x_f32[:].rearrange("p (c x) -> p c x", x=512)
        dsem = "dX0" if tt % 2 == 0 else "dX1"
        src = xT[:, tt * 512:(tt + 1) * 512].rearrange("(c p) x -> p c x", p=128)
        t_x = kb.dma("sync", xfv, src, dsem,
                     prev_reads + [prev_sub.get(par)])
        t_cb = E("vector", lambda e, x_bf=x_bf, x_f32=x_f32: e.tensor_copy(
            x_bf[:], x_f32[:]),
            [t_x, prev_cb.get(0), prev_sq, prev_subx, prev_sqx], "DVE")
        prev_cb[0] = t_cb
        t_sq = E("scalar", lambda e, x_bf=x_bf: e.activation(xsq_bf[:], x_bf[:], AF.Square),
                 [t_cb, prev_sq], "ACT")
        prev_sqx = t_sq
        tok = None
        for c in range(NF):
            tok = E("tensor", lambda e, c=c, xbv=xbv: e.matmul(
                ps[0][0:1, :], ones_colb[:, 0:1], xbv[:, c, :],
                start=(c == 0), stop=(c == NF - 1)),
                [t_cb, t_oc] + prev_reads, "PE" if c == NF - 1 else None)
        t_s1 = tok
        for c in range(NF):
            tok = E("tensor", lambda e, c=c: e.matmul(
                ps[1][0:1, :], ones_colb[:, 0:1], xqv[:, c, :],
                start=(c == 0), stop=(c == NF - 1)),
                [t_sq, t_oc], "PE" if c == NF - 1 else None)
        t_s2 = tok
        t_m = E("vector", lambda e: e.tensor_scalar_mul(statm[0:1, :], ps[0][0:1, :], 1.0 / C),
                [t_s1], "DVE")
        t_ms = E("vector", lambda e: e.tensor_tensor(msq[0:1, :], statm[0:1, :], statm[0:1, :],
                                                     ALU.mult), [t_m], "DVE")
        t_v0 = E("vector", lambda e: e.tensor_scalar_mul(statv[0:1, :], ps[1][0:1, :], 1.0 / C),
                 [t_s2], "DVE")
        t_v = E("vector", lambda e: e.tensor_tensor(statv[0:1, :], statv[0:1, :], msq[0:1, :],
                                                    ALU.subtract), [t_v0, t_ms], "DVE")
        t_ln = E("scalar", lambda e: e.activation(statv[0:1, :], statv[0:1, :], AF.Ln,
                                                  bias=LN_EPS), [t_v, t_c1], "ACT")
        t_rst = E("scalar", lambda e: e.activation(statv[0:1, :], statv[0:1, :], AF.Exp,
                                                   scale=-0.5), [t_ln], "ACT")
        t_mb = E("tensor", lambda e: e.matmul(ps[2][:], ones_rowf[0:1, :], statm[0:1, :],
                                              start=True, stop=True),
                 [t_m, t_or, prev_mbc], "PE")
        t_mbc = E("vector", lambda e: e.tensor_copy(mbc[:], ps[2][:]), [t_mb], "DVE")
        t_rb = E("tensor", lambda e: e.matmul(ps[3][:], ones_rowf[0:1, :], statv[0:1, :],
                                              start=True, stop=True),
                 [t_rst, t_or, prev_rbc], "PE")
        t_rbc = E("vector", lambda e: e.tensor_copy(rbc[:], ps[3][:]), [t_rb], "DVE")
        mb3 = mbc[:].unsqueeze(1).broadcast_to((128, NF, 512))
        rb3 = rbc[:].unsqueeze(1).broadcast_to((128, NF, 512))
        t_sub = E("vector", lambda e, xfv=xfv, xbv=xbv, mb3=mb3: e.tensor_tensor(
            xfv, xbv, mb3, ALU.subtract), [t_cb, t_mbc], "DVE")
        prev_subx = t_sub
        t_xn = E("vector", lambda e, tt=tt, xfv=xfv, rb3=rb3: e.tensor_tensor(
            xnv[:, :, tt * 512:(tt + 1) * 512], xfv, rb3, ALU.mult),
            [t_sub, t_rbc], "DVE")
        xn_done.append(t_xn)
        prev_reads = [t_m, t_v0]
        prev_sub[par] = t_xn
        prev_sq = t_s2
        prev_mbc = t_mbc
        prev_rbc = t_rbc
    A.release()
    dbg_xn_tok = None
    if DEBUG:
        dbg_xn_tok = kb.dma("gpsimd", dbg["xn"][:], xn[:], "dbg0", xn_done)

    A.mark()
    wstag2 = [A.alloc(NF * 256) for _ in range(2)]
    wq_bf = A.alloc(NF * 128, BF16)
    wk_bf = A.alloc(NF * 128, BF16)
    wv_bf = A.alloc(NF * 128, BF16)
    kh = A.alloc(T, BF16)
    vg = A.alloc(16 * 128, BF16)
    qh = A.alloc(OWN, BF16)
    rt1 = A.alloc(512)
    rt2 = A.alloc(512)
    kraw = A.alloc(256, BF16)
    krawb = A.alloc(256, BF16)
    pT = [A.alloc(256, BF16) for _ in range(3)]
    recip = A.alloc(512)
    rep_sb = A.alloc(512)
    y_sb = A.alloc(256, BF16)
    wdp_st2 = [A.alloc(8 * 128)]
    wdp_bf2 = [A.alloc(8 * 64, BF16)]

    wqv = wq_bf[:].rearrange("p (c d) -> p c d", d=256)
    wkv = wk_bf[:].rearrange("p (c d) -> p c d", d=256)
    wvv = wv_bf[:].rearrange("p (c d) -> p c d", d=256)

    y_spills = []
    wfree = {"q": None, "k": None, "v": None}
    stag_free = {}
    kv_read_tok = None
    st_free = {}
    pT_free = {}
    prev_y = None
    prev_rcp = None
    prev_rpe = None
    last_exp = None
    t_wc_last = None
    dbg_war = []

    wdp_cast = {}
    wdp_spill = {}

    def wdp_prep(g):
        for j in range(16):
            s_ = g * 16 + j
            fcw, qi = s_ // 8, s_ % 8
            srcw = wd[qi * 1024:(qi + 1) * 1024, fcw * 128:(fcw + 1) * 128]\
                .rearrange("(c p) d -> p c d", p=128)
            t_wpl = kb.dma("gpsimd", wdp_st2[0][:].rearrange("p (c d) -> p c d", d=128),
                           srcw, "dWQ0" if s_ % 2 == 0 else "dWQ1",
                           [wdp_cast.get(s_ - 1)] + (xn_done if s_ == 0 else []))
            t_wpc = E("vector", lambda e: e.tensor_copy(wdp_bf2[0][:], wdp_st2[0][:]),
                      [t_wpl, wdp_spill.get(s_ - 1)], "DVE")
            wdp_cast[s_] = t_wpc
            t_wps = kb.dma("gpsimd", d_wdb[:, s_ * 1024:(s_ + 1) * 1024],
                           wdp_bf2[0][:], "dWDS0" if s_ % 2 == 0 else "dWDS1", [t_wpc])
            wdp_spill[s_] = t_wps

    for g in range(NG):
        if g > 0:
            wdp_prep(g - 1)
        toks_w = {}
        for wi, (wap, wbf, nm, sem) in enumerate(
                ((wkT, wk_bf, "k", "dW0"), (wvT, wv_bf, "v", "dW1"),
                 (wqT, wq_bf, "q", "dW2"))):
            srcw = wap[:, g * 256:(g + 1) * 256].rearrange("(c p) d -> p c d", p=128)
            sl = (3 * g + wi) % 2
            t_l = kb.dma("sync", wstag2[sl][:].rearrange("p (c d) -> p c d", d=256),
                         srcw, sem,
                         [stag_free.get(sl)] + (xn_done if g == 0 else []))
            t_c = E("vector", lambda e, wbf=wbf, sl=sl: e.tensor_copy(
                wbf[:], wstag2[sl][:]), [t_l, wfree[nm]], "DVE")
            stag_free[sl] = t_c
            toks_w[nm] = t_c
        t_wk, t_wv, t_wq = toks_w["k"], toks_w["v"], toks_w["q"]

        units = [("k", h2, tile) for h2 in range(2) for tile in range(NT)] + \
                [("q", h2, tile) for h2 in range(2) for tile in range(2)]
        rope_done = []
        evac_free = {}
        sw_free = None
        prev_rt = None
        kraw2 = [kraw, krawb]
        kraw_readers = {0: [], 1: []}
        pend = None
        for ui, (kind, h2, tile) in enumerate(units):
            bank = 4 + ui % 2
            wv_ = wkv if kind == "k" else wqv
            tok = None
            for f in range(NF):
                tok = E("tensor", lambda e, f=f, h2=h2, bank=bank, tile=tile, wv_=wv_: e.matmul(
                    ps[bank][:], wv_[:, f, h2 * 128:(h2 + 1) * 128],
                    xnv[:, f, tile * 512:(tile + 1) * 512],
                    start=(f == 0), stop=(f == NF - 1)),
                    [t_wk if kind == "k" else t_wq, xn_done[tile],
                     evac_free.get(bank), kv_read_tok],
                    "PE" if f == NF - 1 else None)
            t_p = tok
            if kind == "k":
                k_last_mm = t_p
            else:
                q_last_mm = t_p
            slot = ui % 2
            t_r = E("scalar", lambda e, bank=bank, slot=slot: e.copy(
                kraw2[slot][:], ps[bank][:]),
                [t_p] + kraw_readers[slot], "ACT")
            evac_free[bank] = t_r

            def emit_rope(uidx, t_evac):
                nonlocal sw_free, prev_rt
                kind_, h2_, tile_ = units[uidx]
                slot_ = uidx % 2
                t_sw = E("tensor", lambda e, slot_=slot_: e.matmul(
                    ps[6][:], pm_sb[:], kraw2[slot_][:], start=True, stop=True),
                    [t_evac, t_pm, sw_free], "PE")
                t_m1 = E("vector", lambda e, tile_=tile_, slot_=slot_: e.tensor_tensor(
                    rt1[:], kraw2[slot_][:], rcos[:, tile_ * 512:(tile_ + 1) * 512],
                    ALU.mult), [t_evac, t_rcc, prev_rt], "DVE")
                t_m2 = E("vector", lambda e, tile_=tile_: e.tensor_tensor(
                    rt2[:], ps[6][:], rsin[:, tile_ * 512:(tile_ + 1) * 512], ALU.mult),
                    [t_sw, t_rsc, prev_rt], "DVE")
                sw_free = t_m2
                if kind_ == "k":
                    dst = kh[:, h2_ * T + tile_ * 512:h2_ * T + (tile_ + 1) * 512]
                else:
                    dst = qh[:, h2_ * OWN + tile_ * 512:h2_ * OWN + (tile_ + 1) * 512]
                t_ro = E("vector", lambda e, dst=dst: e.tensor_tensor(
                    dst, rt1[:], rt2[:], ALU.add),
                    [t_m1, t_m2, kv_read_tok] + dbg_war, "DVE")
                rope_done.append(t_ro)
                prev_rt = t_ro
                kraw_readers[slot_] = [t_sw, t_m1]

            if pend is not None:
                emit_rope(*pend)
            pend = (ui, t_r)
        emit_rope(*pend)
        rope_all = rope_done[-1]

        v_done = []
        for tc in range(16):
            bank = 4 + tc % 2
            tok = None
            for f in range(NF):
                tok = E("tensor", lambda e, f=f, tc=tc, bank=bank: e.matmul(
                    ps[bank][:, 0:256],
                    xnv[:, f, tc * 128:(tc + 1) * 128],
                    wvv[:, f, :],
                    start=(f == 0), stop=(f == NF - 1)),
                    [t_wv, xn_done[tc // 4], evac_free.get(bank), kv_read_tok],
                    "PE" if f == NF - 1 else None)
            t_vp = tok
            v_last_mm = t_vp
            t_ve = E("scalar", lambda e, tc=tc, bank=bank: e.copy(
                vg[:, tc * 256:(tc + 1) * 256], ps[bank][:, 0:256]),
                [t_vp] + kraw_readers[0] + kraw_readers[1] + dbg_war, "ACT")
            evac_free[bank] = t_ve
            v_done.append(t_ve)
        v_all = v_done[-1]

        if DEBUG and g == 0:
            dbg_war.append(kb.dma("gpsimd", dbg["k"][:], kh[:], "dbg1", [rope_all]))
            dbg_war.append(kb.dma("gpsimd", dbg["q"][:], qh[:], "dbg2", [rope_all]))
            dbg_war.append(kb.dma("gpsimd", dbg["v"][:], vg[:], "dbg3", [v_all]))

        group_pe_tail = None
        for h2 in range(2):
            head = 2 * g + h2
            for a in range(2):
                L = list(range(0, 4 * a + 4)) + list(range(8, 8 + 4 * a + 4))
                nL = len(L)
                den_last = None
                pv_last = None
                exps = [None] * nL

                def emit_smask(qi):
                    qpos = L[qi]
                    sbank = qi % 3
                    slot = qi % 3
                    t_s = E("tensor", lambda e, qpos=qpos, h2=h2, a=a, sbank=sbank: e.matmul(
                        ps[[0, 1, 4][sbank]][:],
                        kh[:, h2 * T + qpos * 128:h2 * T + (qpos + 1) * 128],
                        qh[:, h2 * OWN + a * 512:h2 * OWN + (a + 1) * 512],
                        start=True, stop=False),
                        [rope_all, st_free.get(sbank), v_all], "PE")
                    if qpos < 8:
                        rel = qpos - 4 * a
                        mi = rel if 0 <= rel <= 3 else 8
                    else:
                        rel = qpos - 8 - 4 * a
                        mi = 4 + rel if 0 <= rel <= 3 else 8
                    t_mm = E("tensor", lambda e, mi=mi, sbank=sbank: e.matmul(
                        ps[[0, 1, 4][sbank]][:], id_sb[:], mk_sb[:, mi * 512:(mi + 1) * 512],
                        start=False, stop=True),
                        [t_s, t_id, t_mk], "PE")
                    t_e = E("scalar", lambda e, sbank=sbank, slot=slot: e.activation(
                        pT[slot][:], ps[[0, 1, 4][sbank]][:], AF.Exp,
                        bias=EXP_BIAS, scale=EXP_SCALE),
                        [t_mm, t_c0, pT_free.get(slot)], "ACT")
                    st_free[sbank] = t_e
                    exps[qi] = t_e

                def emit_denpv(qi):
                    nonlocal den_last, pv_last, prev_rcp, prev_rpe, prev_y
                    qpos = L[qi]
                    slot = qi % 3
                    t_e = exps[qi]
                    t_d = E("tensor", lambda e, slot=slot, first=(qi == 0),
                            last=(qi == nL - 1): e.matmul(
                        ps[3][0:1, :], ones_colb[:, 0:1], pT[slot][:],
                        start=first, stop=last),
                        [t_e, t_oc] + ([prev_rcp, prev_rpe] if qi == 0 else []), "PE")
                    den_last = t_d
                    t_pv = E("tensor", lambda e, qpos=qpos, h2=h2, slot=slot,
                             first=(qi == 0), last=(qi == nL - 1): e.matmul(
                        ps[2][:], vg[:, qpos * 256 + h2 * 128:qpos * 256 + (h2 + 1) * 128],
                        pT[slot][:], start=first, stop=last),
                        [t_e, v_all] + ([prev_y] if qi == 0 else []), "PE")
                    pT_free[slot] = t_pv
                    pv_last = t_pv

                for qi in range(nL):
                    emit_smask(qi)
                    if qi >= 1:
                        emit_denpv(qi - 1)
                emit_denpv(nL - 1)
                t_rcp = E("vector", lambda e: e.reciprocal(recip[0:1, :], ps[3][0:1, :]),
                          [den_last, prev_rpe], "DVE")
                prev_rcp = t_rcp
                t_rp = E("tensor", lambda e: e.matmul(ps[3][:], ones_rowf[0:1, :],
                                                      recip[0:1, :], start=True, stop=True),
                        [t_rcp, t_or], "PE")
                t_rpe = E("scalar", lambda e: e.copy(rep_sb[:], ps[3][:]),
                          [t_rp, prev_y], "ACT")
                prev_rpe = t_rpe
                t_y = E("vector", lambda e: e.tensor_tensor(y_sb[:], ps[2][:], rep_sb[:],
                                                            ALU.mult),
                        [t_rpe, pv_last, y_spills[-1] if y_spills else None], "DVE")
                prev_y = t_y
                t_ysp = kb.dma("gpsimd",
                               d_y[head * 128:(head + 1) * 128, a * 512:(a + 1) * 512],
                               y_sb[:], "dY", [t_y])
                y_spills.append(t_ysp)
                group_pe_tail = t_rp
        kv_read_tok = group_pe_tail
        wfree = {"q": q_last_mm, "k": k_last_mm, "v": v_last_mm}

    wdp_prep(NG - 1)
    ph2_tail_pe = kv_read_tok
    ph2_tail_dve = prev_y
    ph2_tail_act = prev_rpe
    A.release()

    A.mark()
    y_all = A.alloc(NF * OWN // 2, BF16)
    wstag3 = A.alloc(NF * 512)
    wp_bf = [A.alloc(NF * 256, BF16) for _ in range(2)]
    s_stage = A.alloc(512)
    yav = y_all[:].rearrange("p (c x) -> p c x", x=OWN)
    wdp_last = [wdp_spill[126], wdp_spill[127]]
    t_yb = kb.dma("sync", yav, d_y[:, :].rearrange("(c p) x -> p c x", p=128), "dYB",
                  y_spills + wdp_last + [ph2_tail_pe, ph2_tail_dve, ph2_tail_act,
                                         stag_free.get(0), stag_free.get(1)])
    s_spills = []
    wp_read = {}
    stag3_free = None
    pj_prev = []
    prev_se = None
    for cs4 in range(4):
        par = cs4 % 2
        src = wpT[:, cs4 * 512:(cs4 + 1) * 512].rearrange("(c p) d -> p c d", p=128)
        t_wl = kb.dma("sync", wstag3[:].rearrange("p (c d) -> p c d", d=512), src,
                      "dWP0" if par == 0 else "dWP1",
                      [stag3_free] + ([ph2_tail_pe, ph2_tail_dve, ph2_tail_act,
                                       stag_free.get(0), stag_free.get(1)] + wdp_last
                                      if cs4 == 0 else []))
        t_wc = E("vector", lambda e, par=par: e.tensor_copy(wp_bf[par][:], wstag3[:]),
                 [t_wl, wp_read.get(cs4 - 2), ph2_tail_act if cs4 < 2 else None] +
                 (wdp_last if cs4 == 0 else []), "DVE")
        stag3_free = t_wc
        wpv = wp_bf[par][:].rearrange("p (c d) -> p c d", d=512)
        for cc in range(4):
            c_chunk = cs4 * 4 + cc
            for tt2 in range(2):
                bank = (cc * 2 + tt2) % 4
                tok = None
                for yc in range(NF):
                    tok = E("tensor", lambda e, yc=yc, cc=cc, tt2=tt2, bank=bank,
                            wpv=wpv: e.matmul(
                        ps[bank][:], wpv[:, yc, cc * 128:(cc + 1) * 128],
                        yav[:, yc, tt2 * 512:(tt2 + 1) * 512],
                        start=(yc == 0), stop=(yc == NF - 1)),
                        [t_wc, t_yb] + pj_prev[-4:], "PE" if yc == NF - 1 else None)
                t_pj = tok
                wp_read[cs4] = t_pj
                t_se = E("scalar", lambda e, bank=bank: e.copy(s_stage[:], ps[bank][:]),
                         [t_pj, s_spills[-1] if s_spills else None], "ACT")
                pj_prev.append(t_se)
                prev_se = t_se
                t_ss = kb.dma("gpsimd",
                              d_s[c_chunk * 128:(c_chunk + 1) * 128,
                                  tt2 * 512:(tt2 + 1) * 512],
                              s_stage[:], "dS", [t_se])
                s_spills.append(t_ss)
    ph3_tail_pe = wp_read[3]
    ph3_tail_act = prev_se
    ph3_tail_dve = stag3_free
    ph3_tail_ss = s_spills[-1]
    ph3_tail_wdp = wdp_last
    A.release()

    A.mark()
    gstag2 = [A.alloc(NF * 128) for _ in range(4)]
    wgu_bf = [A.alloc(NF * 256, BF16) for _ in range(2)]
    sg2 = [A.alloc(256), A.alloc(256)]
    h_sb = A.alloc(8 * 512)
    hsq3 = [A.alloc(512) for _ in range(3)]
    hbf3 = [A.alloc(256, BF16) for _ in range(3)]
    thi = A.alloc(256)
    tlo = A.alloc(256)
    tu1 = A.alloc(128)
    tu2 = A.alloc(128)
    tm2 = A.alloc(128)
    tge = A.alloc(512)
    hsp3 = [A.alloc(256, BF16) for _ in range(3)]
    tpev2 = [A.alloc(256, BF16) for _ in range(2)]
    hv = h_sb[:].rearrange("p (t x) -> p t x", x=512)

    hT_spills = []
    gu_read = {}
    gstag_free = {}
    evac_done = {}
    sparse_done = {}
    hdone = {}
    hsp_free = {}
    hbf_free = {}
    tpev_free = {}
    prev_sq = {}
    sg_read = {}
    sq_guard = {}
    tp_read = {}
    sp_prev = None
    tp_bank = 0
    tp_jobs = []
    sp_jobs = []

    def emit_sparsify_chain(ipg2, tc, t_sq, t_hb, slot):
        nonlocal sp_prev
        hsq = hsq3[slot % 3]
        hbf = hbf3[slot % 3]
        q4 = hsq[:].rearrange("p (g x) -> p g x", x=4)
        q2e = hsq[:].rearrange("p (g x) -> p g x", x=2)[:, :, 0]
        q2o = hsq[:].rearrange("p (g x) -> p g x", x=2)[:, :, 1]
        hi2 = thi[:].rearrange("p (g x) -> p g x", x=2)
        lo2 = tlo[:].rearrange("p (g x) -> p g x", x=2)
        t_1 = E("vector", lambda e, q2e=q2e, q2o=q2o: e.tensor_tensor(
            thi[:], q2e, q2o, ALU.max), [t_sq, sp_prev], "DVE")
        t_2 = E("vector", lambda e, q2e=q2e, q2o=q2o: e.tensor_tensor(
            tlo[:], q2e, q2o, ALU.min), [t_sq], "DVE")
        t_3 = E("vector", lambda e, hi2=hi2: e.tensor_tensor(
            tu1[:], hi2[:, :, 0], hi2[:, :, 1], ALU.min), [t_1], "DVE")
        t_4 = E("vector", lambda e, lo2=lo2: e.tensor_tensor(
            tu2[:], lo2[:, :, 0], lo2[:, :, 1], ALU.max), [t_2], "DVE")
        t_5 = E("vector", lambda e: e.tensor_tensor(tm2[:], tu1[:], tu2[:], ALU.max),
                [t_3, t_4], "DVE")
        m2b = tm2[:].unsqueeze(2).broadcast_to((128, 128, 4))
        t_6 = E("vector", lambda e, q4=q4, m2b=m2b: e.tensor_tensor(
            tge[:].rearrange("p (g x) -> p g x", x=4), q4, m2b, ALU.is_ge),
            [t_5], "DVE")
        hsp = hsp3[slot % 3]
        t_7 = E("vector", lambda e, hbf=hbf, hsp=hsp: e.tensor_tensor(
            hsp[:], hbf[:], tge[:], ALU.mult),
            [t_6] + hsp_free.get(slot % 3, []), "DVE")
        sp_prev = t_7
        prev_sq[slot % 3] = t_6
        hbf_free[slot % 3] = t_7
        tp_jobs.append((ipg2, tc, slot, t_7))

    def emit_transpose(ipg2, tc, slot, t_7):
        nonlocal tp_bank
        hsp = hsp3[slot % 3]
        bank = 6 + tp_bank % 2
        tslot = tp_bank % 2
        tpev = tpev2[tslot]
        tp_bank += 1
        tok = None
        for icl in range(4):
            tok = E("tensor", lambda e, icl=icl, bank=bank, hsp=hsp: e.transpose(
                ps[bank][:].bitcast(BF16)[:, icl * 128:(icl + 1) * 128],
                hsp[:, icl * 128:(icl + 1) * 128], id_sb[:]),
                [t_7, t_id, tp_read.get(bank)],
                "PE" if icl == 3 else None)
        t_tp = tok
        t_te = E("vector", lambda e, bank=bank, tpev=tpev: e.tensor_copy(
            tpev[:], ps[bank][:].bitcast(BF16)[:, 0:512]),
            [t_tp, tpev_free.get(tslot)], "DVE")
        tp_read[bank] = t_te
        hsp_free[slot % 3] = [t_te]
        t_hs = kb.dma("gpsimd",
                      d_hT[:].rearrange("p (c x) -> p c x", x=OWN)
                      [:, ipg2 * 4:(ipg2 + 1) * 4, tc * 128:(tc + 1) * 128],
                      tpev[:].rearrange("p (c x) -> p c x", x=128),
                      "dHT0" if tc % 2 == 0 else "dHT1",
                      [t_te, hT_spills[-2] if len(hT_spills) >= 2 else None])
        tpev_free[tslot] = t_hs
        hT_spills.append(t_hs)

    def drain_one_job():
        if len(tp_jobs) > 1:
            emit_transpose(*tp_jobs.pop(0))

    def wgu_load(ip):
        t_gc = None
        for qt in range(4):
            srcw = wgu[:, ip * 512 + qt * 128:ip * 512 + (qt + 1) * 128]\
                .rearrange("(c p) d -> p c d", p=128)
            qs = qt
            t_gl = kb.dma("sync", gstag2[qs][:].rearrange("p (c d) -> p c d", d=128),
                          srcw, ["dG0", "dG1", "dG2", "dG3"][qs],
                          [gstag_free.get(qs)] +
                          (ph3_tail_wdp + [ph3_tail_pe, ph3_tail_act, ph3_tail_dve]
                           if ip == 0 and qt == 0 else []))
            t_gc = E("vector", lambda e, ip=ip, qt=qt, qs=qs: e.tensor_copy(
                wgu_bf[ip % 2][:].rearrange("p (c d) -> p c d", d=512)
                [:, :, qt * 128:(qt + 1) * 128],
                gstag2[qs][:].rearrange("p (c d) -> p c d", d=128)),
                [t_gl, gu_read.get(ip - 2) if qt == 0 else None,
                 ph3_tail_act if ip < 2 and qt == 0 else None,
                 ph3_tail_ss if ip == 0 and qt == 0 else None, t_gc], "DVE")
            gstag_free[qs] = t_gc
        gu_cast[ip] = t_gc

    gu_cast = {}
    wgu_load(0)
    wgu_load(1)
    spslot = 0
    for ip in range(NIP):
        ip2 = ip % 2
        ipg2 = ip // 2
        t_gc = gu_cast[ip]
        par = ip % 2
        gv = wgu_bf[par][:].rearrange("p (c d) -> p c d", d=512)
        for tc in range(8):
            bank = tc % 6
            tok = None
            for f in range(NF):
                tok = E("tensor", lambda e, f=f, tc=tc, bank=bank, gv=gv: e.matmul(
                    ps[bank][:], xnv[:, f, tc * 128:(tc + 1) * 128],
                    gv[:, f, :], start=(f == 0), stop=(f == NF - 1)),
                    [t_gc, evac_done.get(bank), xn_done[tc // 4]] +
                    ([ph3_tail_pe] if ip == 0 and tc == 0 else []),
                    "PE" if f == NF - 1 else None)
            t_gu = tok
            gu_read[ip] = t_gu
            slot = tc % 2
            t_si = E("scalar", lambda e, bank=bank, slot=slot: e.activation(
                sg2[slot][:], ps[bank][:, 0:256], AF.Silu),
                [t_gu, sg_read.get(slot)], "ACT")
            t_h = E("vector", lambda e, bank=bank, tc=tc, ip2=ip2, slot=slot: e.tensor_tensor(
                hv[:, tc, ip2 * 256:(ip2 + 1) * 256], sg2[slot][:],
                ps[bank][:, 256:512], ALU.mult),
                [t_si, sparse_done.get(tc), sq_guard.get(tc)], "DVE")
            sg_read[slot] = t_h
            evac_done[bank] = t_h
            hdone[tc] = t_h
            if ip2 == 1:
                hsq = hsq3[spslot % 3]
                t_sq = E("scalar", lambda e, tc=tc, hsq=hsq: e.activation(
                    hsq[:], hv[:, tc, :], AF.Square),
                    [t_h, prev_sq.get(spslot % 3)], "ACT")
                hbf = hbf3[spslot % 3]
                t_hb = E("vector", lambda e, tc=tc, hbf=hbf: e.tensor_copy(
                    hbf[:], hv[:, tc, :]),
                    [t_h] + ([hbf_free[spslot % 3]] if spslot % 3 in hbf_free else []),
                    "DVE")
                sparse_done[tc] = t_hb
                sq_guard[tc] = t_sq
                sp_jobs.append((ipg2, tc, t_sq, t_hb, spslot))
                spslot += 1
                if len(sp_jobs) > 2:
                    emit_sparsify_chain(*sp_jobs.pop(0))
                drain_one_job()
            else:
                drain_one_job()
        if ip + 2 < NIP:
            wgu_load(ip + 2)
    while sp_jobs:
        emit_sparsify_chain(*sp_jobs.pop(0))
    while tp_jobs:
        emit_transpose(*tp_jobs.pop(0))
    ph4a_tail = [hT_spills[-1], hT_spills[-2], sp_prev] + ph3_tail_wdp + [
                 sg_read.get(0), sg_read.get(1),
                 tp_read.get(6), tp_read.get(7), gu_read.get(31), dbg_xn_tok]
    A.release()
    A.release()

    A.mark()
    hT_sb = A.alloc(NIC * 256, BF16)
    wd_b = [A.alloc(NIC * 64, BF16) for _ in range(2)]
    g2 = [A.alloc(512) for _ in range(2)]
    s_in = [A.alloc(512) for _ in range(2)]
    x_in = [A.alloc(512) for _ in range(2)]
    o_sb = [A.alloc(512) for _ in range(2)]
    hTv = hT_sb[:].rearrange("p (c x) -> p c x", x=512)

    wd_read = {}
    out_toks = []
    gprev = {}
    cm1prev = {}
    hT_read = []
    for sweep in range(2):
        t_ht1 = kb.dma("sync", hTv[:, 0:NIC // 2, :],
                       d_hT[:].rearrange("p (c x) -> p c x", x=OWN)
                       [:, 0:NIC // 2, sweep * 512:(sweep + 1) * 512],
                       "dHTB", ph4a_tail + hT_read)
        t_ht = kb.dma("sync", hTv[:, NIC // 2:NIC, :],
                      d_hT[:].rearrange("p (c x) -> p c x", x=OWN)
                      [:, NIC // 2:NIC, sweep * 512:(sweep + 1) * 512],
                      "dHTB2" if sweep == 0 else "dHTB3", ph4a_tail)
        hT_read = []
        for fc in range(NF):
            par = fc % 2
            if True:
                t_wc = kb.dma("sync", wd_b[par][:],
                              d_wdb[:, fc * 4 * 2048:(fc + 1) * 4 * 2048],
                              ["dWD0", "dWD1"][fc % 2],
                              [wd_read.get(fc - 2)] +
                              (ph4a_tail if sweep == 0 and fc == 0 else []))
                casts = [t_wc]
            wdv = wd_b[par][:].rearrange("p (c d) -> p c d", d=128)
            bank = fc % 2
            tok = None
            for ic in range(NIC):
                tok = E("tensor", lambda e, ic=ic, bank=bank, wdv=wdv: e.matmul(
                    ps[bank][:], wdv[:, ic, :], hTv[:, ic, :],
                    start=(ic == 0), stop=(ic == NIC - 1)),
                    [casts[0], t_wc, t_ht1 if ic < NIC // 2 else t_ht,
                     gprev.get(bank)] +
                    (ph4a_tail if sweep == 0 and fc == 0 else []),
                    "PE" if ic == NIC - 1 else None)
            t_dn = tok
            wd_read[fc] = t_dn
            hT_read.append(t_dn)
            buf = fc % 2
            t_g = E("scalar", lambda e, bank=bank, buf=buf: e.activation(
                g2[buf][:], ps[bank][:], AF.Silu),
                [t_dn, cm1prev.get(buf)], "ACT")
            gprev[bank] = t_g
            t_sl = kb.dma("gpsimd", s_in[buf],
                          d_s[fc * 128:(fc + 1) * 128, sweep * 512:(sweep + 1) * 512],
                          "dSI0" if buf == 0 else "dSI1",
                          [s_spills[-1], out_toks[-2] if len(out_toks) >= 2 else None])
            t_xl = kb.dma("gpsimd", x_in[buf],
                          xT[fc * 128:(fc + 1) * 128, sweep * 512:(sweep + 1) * 512],
                          "dXI0" if buf == 0 else "dXI1",
                          [out_toks[-2] if len(out_toks) >= 2 else None])
            t_cm1 = E("vector", lambda e, buf=buf: e.tensor_tensor(
                o_sb[buf], s_in[buf], g2[buf][:], ALU.mult),
                [t_g, t_sl, out_toks[-2] if len(out_toks) >= 2 else None], "DVE")
            cm1prev[buf] = t_cm1
            t_cm2 = E("vector", lambda e, buf=buf: e.tensor_tensor(
                o_sb[buf], o_sb[buf], x_in[buf], ALU.add), [t_cm1, t_xl], "DVE")
            t_o = kb.dma("gpsimd",
                         out[fc * 128:(fc + 1) * 128, sweep * 512:(sweep + 1) * 512],
                         o_sb[buf], "dOUT0" if buf == 0 else "dOUT1", [t_cm2])
            out_toks.append(t_o)
    E("vector", lambda e: e.memset(g2[0][0:1, 0:1], 0.0),
      out_toks + [gprev.get(0), gprev.get(1)], "DVE")
    A.release()

    sems = {}
    for s in kb.sem_names:
        sems[s] = st.enter_context(nc.semaphore(s))
    blk = st.enter_context(nc.Block())
    kb.finalize(blk, sems)
    st.close()
    return nc



def _host_stage(x_full, w_atten, w_proj, w_gate, w_up, w_down):
    wq = w_atten[0:C, :]
    wk = w_atten[C:2 * C, :]
    wv = w_atten[2 * C:3 * C, :]
    wqT = np.ascontiguousarray(wq.T)
    wkT = np.ascontiguousarray(wk.T)
    wvT = np.ascontiguousarray(wv.T)
    wpT = np.ascontiguousarray(w_proj.T)
    wgu = np.empty((C, 2 * I), dtype=np.float32)
    for ip in range(NIP):
        wgu[:, ip * 512:ip * 512 + 256] = w_gate[:, ip * 256:(ip + 1) * 256]
        wgu[:, ip * 512 + 256:(ip + 1) * 512] = w_up[:, ip * 256:(ip + 1) * 256]
    pm = np.zeros((128, 128), dtype=ml_dtypes.bfloat16)
    for i in range(64):
        pm[2 * i + 1, 2 * i] = -1.0
        pm[2 * i, 2 * i + 1] = 1.0
    ident = np.eye(128, dtype=ml_dtypes.bfloat16)
    theta = 1.0 / 10000.0 ** (np.arange(0, D, 2, dtype=np.float64) / D)

    in_maps = []
    for core in range(8):
        b, p = core // 2, core % 2
        blocks = [2 * j + p for j in range(8)] + [2 * j + 1 - p for j in range(8)]
        perm = np.concatenate([np.arange(128 * g, 128 * g + 128) for g in blocks])
        xTc = np.ascontiguousarray(x_full[b].T[:, perm])
        ang = np.outer(theta, perm.astype(np.float64))
        ropeCc = np.repeat(np.cos(ang), 2, axis=0).astype(np.float32)
        ropeSc = np.repeat(np.sin(ang), 2, axis=0).astype(np.float32)
        mk = np.zeros((128, 9 * 512), dtype=np.float32)
        for rel in range(4):
            m = np.zeros((128, 512), dtype=np.float32)
            for u in range(4):
                if rel > u:
                    m[:, u * 128:(u + 1) * 128] = -1e30
                elif rel == u:
                    blkm = np.where(np.tril(np.ones((128, 128), dtype=bool), -1),
                                    np.float32(-1e30), np.float32(0.0))
                    m[:, u * 128:(u + 1) * 128] = blkm
            mk[:, rel * 512:(rel + 1) * 512] = m
        for rel in range(4):
            m = np.zeros((128, 512), dtype=np.float32)
            for u in range(4):
                valid = (rel < u) if p == 0 else (rel <= u)
                if not valid:
                    m[:, u * 128:(u + 1) * 128] = -1e30
            mk[:, (4 + rel) * 512:(5 + rel) * 512] = m
        in_maps.append({
            "xT": xTc,
            "wqT": wqT, "wkT": wkT, "wvT": wvT, "wpT": wpT,
            "wgu": wgu, "wd": np.ascontiguousarray(w_down),
            "ropeC": ropeCc, "ropeS": ropeSc,
            "pmat": pm, "ident": ident,
            "masks": mk.astype(ml_dtypes.bfloat16),
        })
    return in_maps


def kernel(L0_attn_x, L0_attn_w_atten, L0_attn_w_proj,
           L0_ffn_w_gate, L0_ffn_w_up, L0_ffn_w_down):
    x = np.asarray(L0_attn_x, dtype=np.float32)
    in_maps = _host_stage(x, np.asarray(L0_attn_w_atten, np.float32),
                          np.asarray(L0_attn_w_proj, np.float32),
                          np.asarray(L0_ffn_w_gate, np.float32),
                          np.asarray(L0_ffn_w_up, np.float32),
                          np.asarray(L0_ffn_w_down, np.float32))
    nc = build_nc()
    res = run_bass_kernel_spmd(nc, in_maps, core_ids=list(range(8)))
    outf = np.empty((B, T, C), dtype=np.float32)
    for core in range(8):
        b, p = core // 2, core % 2
        oc = np.asarray(res.results[core]["out"])
        for j in range(8):
            gb = 2 * j + p
            outf[b, gb * 128:(gb + 1) * 128, :] = oc[:, j * 128:(j + 1) * 128].T
    return outf


# version 16
# speedup vs baseline: 1.1982x; 1.1982x over previous
import numpy as np
import ml_dtypes
from contextlib import ExitStack

import concourse.bass as bass
import concourse.mybir as mybir
from concourse.bass_utils import run_bass_kernel_spmd

F32 = mybir.dt.float32
BF16 = mybir.dt.bfloat16
ALU = mybir.AluOpType
AF = mybir.ActivationFunctionType

B, T, C, H, D, I = 4, 2048, 2048, 16, 128, 8192
NF = C // 128
NT = T // 512
OWN = T // 2
NG = 8
NIP = I // 256
NIC = I // 128
EXP_BIAS = -2.0
EXP_SCALE = float(1.0 / np.sqrt(128.0))
LN_EPS = 1e-5

ENGINES = ("sync", "tensor", "scalar", "vector", "gpsimd")
DEBUG = False


class KB:

    def __init__(self, nc):
        self.nc = nc
        self.prog = {e: [] for e in ENGINES}
        self.model = {}
        self.watermark = {e: {} for e in ENGINES}
        self.sem_names = []

    def sem(self, name):
        if name not in self.model:
            self.sem_names.append(name)
            self.model[name] = 0
        return name

    def emit(self, eng, fn, waits=(), inc=None, inc_by=1):
        wl = []
        wm = self.watermark[eng]
        coal = {}
        for t in waits:
            if t is None:
                continue
            s, v = t
            coal[s] = max(coal.get(s, 0), v)
        for s, v in coal.items():
            if v > wm.get(s, 0):
                wm[s] = v
                wl.append((s, v))
        tok = None
        if inc is not None:
            self.sem(inc)
            self.model[inc] += inc_by
            tok = (inc, self.model[inc])
        self.prog[eng].append((tuple(wl), fn, (inc, inc_by) if inc else None))
        return tok

    def dma(self, eng, out, in_, sem, waits=()):
        return self.emit(eng, lambda e: e.dma_start(out=out, in_=in_),
                         waits, inc=sem, inc_by=16)

    def finalize(self, block, sems):
        for eng in ENGINES:
            ops = self.prog[eng]
            if not ops:
                continue

            def make(ops):
                def body(handle):
                    for waits, fn, inc in ops:
                        for s, v in waits:
                            handle.wait_ge(sems[s], v)
                        ins = fn(handle)
                        if inc is not None:
                            s, by = inc
                            ins.then_inc(sems[s], by)
                return body
            getattr(block, eng)(make(ops))


class Arena:

    def __init__(self, ap, cols):
        self.ap = ap
        self.cols = cols
        self.ptr = 0
        self.marks = []
        self.peak = 0

    def alloc(self, f32cols, dtype=F32):
        f32cols = int(f32cols)
        a = self.ptr
        assert a + f32cols <= self.cols, f"arena overflow {a}+{f32cols}>{self.cols}"
        self.ptr = a + f32cols
        self.peak = max(self.peak, self.ptr)
        v = self.ap[:, a:a + f32cols]
        if dtype == BF16:
            v = v.bitcast(BF16)
        return v

    def mark(self):
        self.marks.append(self.ptr)

    def release(self):
        self.ptr = self.marks.pop()


def build_nc():
    nc = bass.Bass("TRN2", target_bir_lowering=False)

    xT = nc.dram_tensor("xT", [C, T], F32, kind="ExternalInput").ap()
    wqT = nc.dram_tensor("wqT", [C, C], F32, kind="ExternalInput").ap()
    wkT = nc.dram_tensor("wkT", [C, C], F32, kind="ExternalInput").ap()
    wvT = nc.dram_tensor("wvT", [C, C], F32, kind="ExternalInput").ap()
    wpT = nc.dram_tensor("wpT", [C, C], F32, kind="ExternalInput").ap()
    wgu = nc.dram_tensor("wgu", [C, 2 * I], F32, kind="ExternalInput").ap()
    wd = nc.dram_tensor("wd", [I, C], F32, kind="ExternalInput").ap()
    ropeC = nc.dram_tensor("ropeC", [128, T], F32, kind="ExternalInput").ap()
    ropeS = nc.dram_tensor("ropeS", [128, T], F32, kind="ExternalInput").ap()
    pmat = nc.dram_tensor("pmat", [128, 128], BF16, kind="ExternalInput").ap()
    ident = nc.dram_tensor("ident", [128, 128], BF16, kind="ExternalInput").ap()
    masks = nc.dram_tensor("masks", [128, 9 * 512], BF16, kind="ExternalInput").ap()
    out = nc.dram_tensor("out", [C, OWN], F32, kind="ExternalOutput").ap()

    d_y = nc.dram_tensor("d_y", [C, OWN], BF16).ap()
    d_s = nc.dram_tensor("d_s", [C, OWN], F32).ap()
    d_hT = nc.dram_tensor("d_hT", [128, NIC * OWN], BF16).ap()
    d_wdb = nc.dram_tensor("d_wdb", [128, NIC * C], BF16).ap()
    dbg = {}
    if DEBUG:
        dbg["xn"] = nc.dram_tensor("dbg_xn", [128, NF * T], BF16).ap()
        dbg["k"] = nc.dram_tensor("dbg_k", [128, 2 * T], BF16).ap()
        dbg["q"] = nc.dram_tensor("dbg_q", [128, 2 * OWN], BF16).ap()
        dbg["v"] = nc.dram_tensor("dbg_v", [128, 16 * 256], BF16).ap()
        dbg["hsp"] = nc.dram_tensor("dbg_hsp", [128, 8 * 1024], BF16).ap()

    st = ExitStack()
    ARENA_COLS = 49088
    arena_t = st.enter_context(nc.sbuf_tensor("arena", [128, ARENA_COLS], F32))
    A = Arena(arena_t.ap(), ARENA_COLS)

    rcos = A.alloc(T // 2, BF16)
    rsin = A.alloc(T // 2, BF16)
    pm_sb = A.alloc(64, BF16)
    id_sb = A.alloc(64, BF16)
    mk_sb = A.alloc(9 * 256, BF16)
    ones_colb = A.alloc(1, BF16)
    ones_rowf = A.alloc(128)
    consts = A.alloc(2)
    A.mark()
    xn = A.alloc(NF * T // 2, BF16)

    nc.const_aps.register(F32, EXP_BIAS, consts[:, 0:1])
    nc.const_aps.register(F32, LN_EPS, consts[:, 1:2])

    ps = [st.enter_context(nc.psum_tensor(f"ps{i}", [128, 512], F32)).ap()
          for i in range(8)]

    kb = KB(nc)
    E = kb.emit
    for s in ["dRC", "dRS", "dPM", "dID", "dMK", "dX0", "dX1",
              "dW0", "dW1", "dW2", "dY", "dYB", "dWP0", "dWP1",
              "dG0", "dG1", "dG2", "dG3", "dS", "dHT0", "dHT1", "dHTB", "dHTB2", "dHTB3",
              "dWD0", "dWD1", "dWQ0", "dWQ1", "dWDS0", "dWDS1", "dWDB0", "dWDB1",
              "dSI0", "dSI1", "dXI0", "dXI1", "dOUT0", "dOUT1",
              "PE", "ACT", "DVE", "POOL",
              "dbg0", "dbg1", "dbg2", "dbg3", "dbg4"]:
        kb.sem(s)

    xnv = xn[:].rearrange("p (c x) -> p c x", x=T)

    A.mark()
    cst32 = A.alloc(2 * T)
    t_rc32 = kb.dma("sync", cst32[:, 0:T], ropeC[:], "dRC")
    t_rs32 = kb.dma("sync", cst32[:, T:2 * T], ropeS[:], "dRS")
    t_rcc = E("vector", lambda e: e.tensor_copy(rcos[:], cst32[:, 0:T]), [t_rc32], "DVE")
    t_rsc = E("vector", lambda e: e.tensor_copy(rsin[:], cst32[:, T:2 * T]), [t_rs32], "DVE")
    t_pm = kb.dma("sync", pm_sb[:], pmat[:], "dPM")
    t_id = kb.dma("sync", id_sb[:], ident[:], "dID")
    t_mk = kb.dma("sync", mk_sb[:], masks[:], "dMK")
    t_c0 = E("vector", lambda e: e.memset(consts[:, 0:1], EXP_BIAS), [], "DVE")
    t_c1 = E("vector", lambda e: e.memset(consts[:, 1:2], LN_EPS), [], "DVE")
    t_oc = E("vector", lambda e: e.memset(ones_colb[:, 0:1], 1.0), [], "DVE")
    t_or = E("vector", lambda e: e.memset(ones_rowf[0:1, :], 1.0), [], "DVE")
    A.release()

    A.mark()
    x_f32_2 = [A.alloc(NF * 512) for _ in range(2)]
    x_bf_1 = A.alloc(NF * 256, BF16)
    x_bf_2 = [x_bf_1, x_bf_1]
    xsq_bf = A.alloc(NF * 256, BF16)
    statm = A.alloc(512)
    statv = A.alloc(512)
    mbc = A.alloc(512)
    rbc = A.alloc(512)
    msq = A.alloc(512)

    xqv = xsq_bf[:].rearrange("p (c x) -> p c x", x=512)
    xn_done = []
    prev_reads = [t_rcc, t_rsc]
    prev_sub = {}
    prev_sq = None
    prev_subx = None
    prev_sqx = None
    prev_cb = {}
    prev_mbc = None
    prev_rbc = None
    for tt in range(NT):
        par = tt % 2
        x_f32 = x_f32_2[par]
        x_bf = x_bf_2[par]
        xbv = x_bf[:].rearrange("p (c x) -> p c x", x=512)
        xfv = x_f32[:].rearrange("p (c x) -> p c x", x=512)
        dsem = "dX0" if tt % 2 == 0 else "dX1"
        src = xT[:, tt * 512:(tt + 1) * 512].rearrange("(c p) x -> p c x", p=128)
        t_x = kb.dma("sync", xfv, src, dsem,
                     prev_reads + [prev_sub.get(par)])
        t_cb = E("vector", lambda e, x_bf=x_bf, x_f32=x_f32: e.tensor_copy(
            x_bf[:], x_f32[:]),
            [t_x, prev_cb.get(0), prev_sq, prev_subx, prev_sqx], "DVE")
        prev_cb[0] = t_cb
        t_sq = E("scalar", lambda e, x_bf=x_bf: e.activation(xsq_bf[:], x_bf[:], AF.Square),
                 [t_cb, prev_sq], "ACT")
        prev_sqx = t_sq
        tok = None
        for c in range(NF):
            tok = E("tensor", lambda e, c=c, xbv=xbv: e.matmul(
                ps[0][0:1, :], ones_colb[:, 0:1], xbv[:, c, :],
                start=(c == 0), stop=(c == NF - 1)),
                [t_cb, t_oc] + prev_reads, "PE" if c == NF - 1 else None)
        t_s1 = tok
        for c in range(NF):
            tok = E("tensor", lambda e, c=c: e.matmul(
                ps[1][0:1, :], ones_colb[:, 0:1], xqv[:, c, :],
                start=(c == 0), stop=(c == NF - 1)),
                [t_sq, t_oc], "PE" if c == NF - 1 else None)
        t_s2 = tok
        t_m = E("vector", lambda e: e.tensor_scalar_mul(statm[0:1, :], ps[0][0:1, :], 1.0 / C),
                [t_s1], "DVE")
        t_ms = E("vector", lambda e: e.tensor_tensor(msq[0:1, :], statm[0:1, :], statm[0:1, :],
                                                     ALU.mult), [t_m], "DVE")
        t_v0 = E("vector", lambda e: e.tensor_scalar_mul(statv[0:1, :], ps[1][0:1, :], 1.0 / C),
                 [t_s2], "DVE")
        t_v = E("vector", lambda e: e.tensor_tensor(statv[0:1, :], statv[0:1, :], msq[0:1, :],
                                                    ALU.subtract), [t_v0, t_ms], "DVE")
        t_ln = E("scalar", lambda e: e.activation(statv[0:1, :], statv[0:1, :], AF.Ln,
                                                  bias=LN_EPS), [t_v, t_c1], "ACT")
        t_rst = E("scalar", lambda e: e.activation(statv[0:1, :], statv[0:1, :], AF.Exp,
                                                   scale=-0.5), [t_ln], "ACT")
        t_mb = E("tensor", lambda e: e.matmul(ps[2][:], ones_rowf[0:1, :], statm[0:1, :],
                                              start=True, stop=True),
                 [t_m, t_or, prev_mbc], "PE")
        t_mbc = E("vector", lambda e: e.tensor_copy(mbc[:], ps[2][:]), [t_mb], "DVE")
        t_rb = E("tensor", lambda e: e.matmul(ps[3][:], ones_rowf[0:1, :], statv[0:1, :],
                                              start=True, stop=True),
                 [t_rst, t_or, prev_rbc], "PE")
        t_rbc = E("vector", lambda e: e.tensor_copy(rbc[:], ps[3][:]), [t_rb], "DVE")
        mb3 = mbc[:].unsqueeze(1).broadcast_to((128, NF, 512))
        rb3 = rbc[:].unsqueeze(1).broadcast_to((128, NF, 512))
        t_sub = E("vector", lambda e, xfv=xfv, xbv=xbv, mb3=mb3: e.tensor_tensor(
            xfv, xbv, mb3, ALU.subtract), [t_cb, t_mbc], "DVE")
        prev_subx = t_sub
        t_xn = E("vector", lambda e, tt=tt, xfv=xfv, rb3=rb3: e.tensor_tensor(
            xnv[:, :, tt * 512:(tt + 1) * 512], xfv, rb3, ALU.mult),
            [t_sub, t_rbc], "DVE")
        xn_done.append(t_xn)
        prev_reads = [t_m, t_v0]
        prev_sub[par] = t_xn
        prev_sq = t_s2
        prev_mbc = t_mbc
        prev_rbc = t_rbc
    A.release()
    dbg_xn_tok = None
    if DEBUG:
        dbg_xn_tok = kb.dma("gpsimd", dbg["xn"][:], xn[:], "dbg0", xn_done)

    A.mark()
    wstag2 = [A.alloc(NF * 256) for _ in range(2)]
    wq_bf = A.alloc(NF * 128, BF16)
    wk_bf = A.alloc(NF * 128, BF16)
    wv_bf = A.alloc(NF * 128, BF16)
    kh = A.alloc(T, BF16)
    vg = A.alloc(16 * 128, BF16)
    qh = A.alloc(OWN, BF16)
    rt1 = A.alloc(512)
    rt2 = A.alloc(512)
    kraw = A.alloc(256, BF16)
    krawb = A.alloc(256, BF16)
    pT = [A.alloc(256, BF16) for _ in range(3)]
    recip = A.alloc(512)
    rep_sb = A.alloc(512)
    y_sb = A.alloc(256, BF16)

    wqv = wq_bf[:].rearrange("p (c d) -> p c d", d=256)
    wkv = wk_bf[:].rearrange("p (c d) -> p c d", d=256)
    wvv = wv_bf[:].rearrange("p (c d) -> p c d", d=256)

    y_spills = []
    wfree = {"q": None, "k": None, "v": None}
    stag_free = {}
    kv_read_tok = None
    st_free = {}
    pT_free = {}
    prev_y = None
    prev_rcp = None
    prev_rpe = None
    last_exp = None
    t_wc_last = None
    dbg_war = []

    for g in range(NG):
        toks_w = {}
        for wi, (wap, wbf, nm, sem) in enumerate(
                ((wkT, wk_bf, "k", "dW0"), (wvT, wv_bf, "v", "dW1"),
                 (wqT, wq_bf, "q", "dW2"))):
            srcw = wap[:, g * 256:(g + 1) * 256].rearrange("(c p) d -> p c d", p=128)
            sl = (3 * g + wi) % 2
            t_l = kb.dma("sync", wstag2[sl][:].rearrange("p (c d) -> p c d", d=256),
                         srcw, sem,
                         [stag_free.get(sl)] + (xn_done if g == 0 else []))
            t_c = E("vector", lambda e, wbf=wbf, sl=sl: e.tensor_copy(
                wbf[:], wstag2[sl][:]), [t_l, wfree[nm]], "DVE")
            stag_free[sl] = t_c
            toks_w[nm] = t_c
        t_wk, t_wv, t_wq = toks_w["k"], toks_w["v"], toks_w["q"]

        units = [("k", h2, tile) for h2 in range(2) for tile in range(NT)] + \
                [("q", h2, tile) for h2 in range(2) for tile in range(2)]
        rope_done = []
        evac_free = {}
        sw_free = None
        prev_rt = None
        kraw2 = [kraw, krawb]
        kraw_readers = {0: [], 1: []}
        pend = None
        for ui, (kind, h2, tile) in enumerate(units):
            bank = 4 + ui % 2
            wv_ = wkv if kind == "k" else wqv
            tok = None
            for f in range(NF):
                tok = E("tensor", lambda e, f=f, h2=h2, bank=bank, tile=tile, wv_=wv_: e.matmul(
                    ps[bank][:], wv_[:, f, h2 * 128:(h2 + 1) * 128],
                    xnv[:, f, tile * 512:(tile + 1) * 512],
                    start=(f == 0), stop=(f == NF - 1)),
                    [t_wk if kind == "k" else t_wq, xn_done[tile],
                     evac_free.get(bank), kv_read_tok],
                    "PE" if f == NF - 1 else None)
            t_p = tok
            if kind == "k":
                k_last_mm = t_p
            else:
                q_last_mm = t_p
            slot = ui % 2
            t_r = E("scalar", lambda e, bank=bank, slot=slot: e.copy(
                kraw2[slot][:], ps[bank][:]),
                [t_p] + kraw_readers[slot], "ACT")
            evac_free[bank] = t_r

            def emit_rope(uidx, t_evac):
                nonlocal sw_free, prev_rt
                kind_, h2_, tile_ = units[uidx]
                slot_ = uidx % 2
                t_sw = E("tensor", lambda e, slot_=slot_: e.matmul(
                    ps[6][:], pm_sb[:], kraw2[slot_][:], start=True, stop=True),
                    [t_evac, t_pm, sw_free], "PE")
                t_m1 = E("vector", lambda e, tile_=tile_, slot_=slot_: e.tensor_tensor(
                    rt1[:], kraw2[slot_][:], rcos[:, tile_ * 512:(tile_ + 1) * 512],
                    ALU.mult), [t_evac, t_rcc, prev_rt], "DVE")
                t_m2 = E("vector", lambda e, tile_=tile_: e.tensor_tensor(
                    rt2[:], ps[6][:], rsin[:, tile_ * 512:(tile_ + 1) * 512], ALU.mult),
                    [t_sw, t_rsc, prev_rt], "DVE")
                sw_free = t_m2
                if kind_ == "k":
                    dst = kh[:, h2_ * T + tile_ * 512:h2_ * T + (tile_ + 1) * 512]
                else:
                    dst = qh[:, h2_ * OWN + tile_ * 512:h2_ * OWN + (tile_ + 1) * 512]
                t_ro = E("vector", lambda e, dst=dst: e.tensor_tensor(
                    dst, rt1[:], rt2[:], ALU.add),
                    [t_m1, t_m2, kv_read_tok] + dbg_war, "DVE")
                rope_done.append(t_ro)
                prev_rt = t_ro
                kraw_readers[slot_] = [t_sw, t_m1]

            if pend is not None:
                emit_rope(*pend)
            pend = (ui, t_r)
        emit_rope(*pend)
        rope_all = rope_done[-1]

        v_done = []
        for tc in range(16):
            bank = 4 + tc % 2
            tok = None
            for f in range(NF):
                tok = E("tensor", lambda e, f=f, tc=tc, bank=bank: e.matmul(
                    ps[bank][:, 0:256],
                    xnv[:, f, tc * 128:(tc + 1) * 128],
                    wvv[:, f, :],
                    start=(f == 0), stop=(f == NF - 1)),
                    [t_wv, xn_done[tc // 4], evac_free.get(bank), kv_read_tok],
                    "PE" if f == NF - 1 else None)
            t_vp = tok
            v_last_mm = t_vp
            t_ve = E("scalar", lambda e, tc=tc, bank=bank: e.copy(
                vg[:, tc * 256:(tc + 1) * 256], ps[bank][:, 0:256]),
                [t_vp] + kraw_readers[0] + kraw_readers[1] + dbg_war, "ACT")
            evac_free[bank] = t_ve
            v_done.append(t_ve)
        v_all = v_done[-1]

        if DEBUG and g == 0:
            dbg_war.append(kb.dma("gpsimd", dbg["k"][:], kh[:], "dbg1", [rope_all]))
            dbg_war.append(kb.dma("gpsimd", dbg["q"][:], qh[:], "dbg2", [rope_all]))
            dbg_war.append(kb.dma("gpsimd", dbg["v"][:], vg[:], "dbg3", [v_all]))

        group_pe_tail = None
        for h2 in range(2):
            head = 2 * g + h2
            for a in range(2):
                L = list(range(0, 4 * a + 4)) + list(range(8, 8 + 4 * a + 4))
                nL = len(L)
                den_last = None
                pv_last = None
                exps = [None] * nL

                def emit_smask(qi):
                    qpos = L[qi]
                    sbank = qi % 3
                    slot = qi % 3
                    t_s = E("tensor", lambda e, qpos=qpos, h2=h2, a=a, sbank=sbank: e.matmul(
                        ps[[0, 1, 4][sbank]][:],
                        kh[:, h2 * T + qpos * 128:h2 * T + (qpos + 1) * 128],
                        qh[:, h2 * OWN + a * 512:h2 * OWN + (a + 1) * 512],
                        start=True, stop=False),
                        [rope_all, st_free.get(sbank), v_all], "PE")
                    if qpos < 8:
                        rel = qpos - 4 * a
                        mi = rel if 0 <= rel <= 3 else 8
                    else:
                        rel = qpos - 8 - 4 * a
                        mi = 4 + rel if 0 <= rel <= 3 else 8
                    t_mm = E("tensor", lambda e, mi=mi, sbank=sbank: e.matmul(
                        ps[[0, 1, 4][sbank]][:], id_sb[:], mk_sb[:, mi * 512:(mi + 1) * 512],
                        start=False, stop=True),
                        [t_s, t_id, t_mk], "PE")
                    t_e = E("scalar", lambda e, sbank=sbank, slot=slot: e.activation(
                        pT[slot][:], ps[[0, 1, 4][sbank]][:], AF.Exp,
                        bias=EXP_BIAS, scale=EXP_SCALE),
                        [t_mm, t_c0, pT_free.get(slot)], "ACT")
                    st_free[sbank] = t_e
                    exps[qi] = t_e

                def emit_denpv(qi):
                    nonlocal den_last, pv_last, prev_rcp, prev_rpe, prev_y
                    qpos = L[qi]
                    slot = qi % 3
                    t_e = exps[qi]
                    t_d = E("tensor", lambda e, slot=slot, first=(qi == 0),
                            last=(qi == nL - 1): e.matmul(
                        ps[3][0:1, :], ones_colb[:, 0:1], pT[slot][:],
                        start=first, stop=last),
                        [t_e, t_oc] + ([prev_rcp, prev_rpe] if qi == 0 else []), "PE")
                    den_last = t_d
                    t_pv = E("tensor", lambda e, qpos=qpos, h2=h2, slot=slot,
                             first=(qi == 0), last=(qi == nL - 1): e.matmul(
                        ps[2][:], vg[:, qpos * 256 + h2 * 128:qpos * 256 + (h2 + 1) * 128],
                        pT[slot][:], start=first, stop=last),
                        [t_e, v_all] + ([prev_y] if qi == 0 else []), "PE")
                    pT_free[slot] = t_pv
                    pv_last = t_pv

                for qi in range(nL):
                    emit_smask(qi)
                    if qi >= 1:
                        emit_denpv(qi - 1)
                emit_denpv(nL - 1)
                t_rcp = E("vector", lambda e: e.reciprocal(recip[0:1, :], ps[3][0:1, :]),
                          [den_last, prev_rpe], "DVE")
                prev_rcp = t_rcp
                t_rp = E("tensor", lambda e: e.matmul(ps[3][:], ones_rowf[0:1, :],
                                                      recip[0:1, :], start=True, stop=True),
                        [t_rcp, t_or], "PE")
                t_rpe = E("scalar", lambda e: e.copy(rep_sb[:], ps[3][:]),
                          [t_rp, prev_y], "ACT")
                prev_rpe = t_rpe
                t_y = E("vector", lambda e: e.tensor_tensor(y_sb[:], ps[2][:], rep_sb[:],
                                                            ALU.mult),
                        [t_rpe, pv_last, y_spills[-1] if y_spills else None], "DVE")
                prev_y = t_y
                t_ysp = kb.dma("gpsimd",
                               d_y[head * 128:(head + 1) * 128, a * 512:(a + 1) * 512],
                               y_sb[:], "dY", [t_y])
                y_spills.append(t_ysp)
                group_pe_tail = t_rp
        kv_read_tok = group_pe_tail
        wfree = {"q": q_last_mm, "k": k_last_mm, "v": v_last_mm}

    ph2_tail_pe = kv_read_tok
    ph2_tail_dve = prev_y
    ph2_tail_act = prev_rpe
    A.release()

    A.mark()
    y_all = A.alloc(NF * OWN // 2, BF16)
    wstag3 = A.alloc(NF * 512)
    wp_bf = [A.alloc(NF * 256, BF16) for _ in range(2)]
    s_stage = A.alloc(512)
    yav = y_all[:].rearrange("p (c x) -> p c x", x=OWN)
    wdp_last = []
    t_yb = kb.dma("sync", yav, d_y[:, :].rearrange("(c p) x -> p c x", p=128), "dYB",
                  y_spills + wdp_last + [ph2_tail_pe, ph2_tail_dve, ph2_tail_act,
                                         stag_free.get(0), stag_free.get(1)])
    s_spills = []
    wp_read = {}
    stag3_free = None
    pj_prev = []
    prev_se = None
    for cs4 in range(4):
        par = cs4 % 2
        src = wpT[:, cs4 * 512:(cs4 + 1) * 512].rearrange("(c p) d -> p c d", p=128)
        t_wl = kb.dma("sync", wstag3[:].rearrange("p (c d) -> p c d", d=512), src,
                      "dWP0" if par == 0 else "dWP1",
                      [stag3_free] + ([ph2_tail_pe, ph2_tail_dve, ph2_tail_act,
                                       stag_free.get(0), stag_free.get(1)] + wdp_last
                                      if cs4 == 0 else []))
        t_wc = E("vector", lambda e, par=par: e.tensor_copy(wp_bf[par][:], wstag3[:]),
                 [t_wl, wp_read.get(cs4 - 2), ph2_tail_act if cs4 < 2 else None] +
                 (wdp_last if cs4 == 0 else []), "DVE")
        stag3_free = t_wc
        wpv = wp_bf[par][:].rearrange("p (c d) -> p c d", d=512)
        for cc in range(4):
            c_chunk = cs4 * 4 + cc
            for tt2 in range(2):
                bank = (cc * 2 + tt2) % 4
                tok = None
                for yc in range(NF):
                    tok = E("tensor", lambda e, yc=yc, cc=cc, tt2=tt2, bank=bank,
                            wpv=wpv: e.matmul(
                        ps[bank][:], wpv[:, yc, cc * 128:(cc + 1) * 128],
                        yav[:, yc, tt2 * 512:(tt2 + 1) * 512],
                        start=(yc == 0), stop=(yc == NF - 1)),
                        [t_wc, t_yb] + pj_prev[-4:], "PE" if yc == NF - 1 else None)
                t_pj = tok
                wp_read[cs4] = t_pj
                t_se = E("scalar", lambda e, bank=bank: e.copy(s_stage[:], ps[bank][:]),
                         [t_pj, s_spills[-1] if s_spills else None], "ACT")
                pj_prev.append(t_se)
                prev_se = t_se
                t_ss = kb.dma("gpsimd",
                              d_s[c_chunk * 128:(c_chunk + 1) * 128,
                                  tt2 * 512:(tt2 + 1) * 512],
                              s_stage[:], "dS", [t_se])
                s_spills.append(t_ss)
    ph3_tail_pe = wp_read[3]
    ph3_tail_act = prev_se
    ph3_tail_dve = stag3_free
    ph3_tail_ss = s_spills[-1]
    ph3_tail_wdp = wdp_last
    A.release()

    A.mark()
    gstag2 = [A.alloc(NF * 128) for _ in range(4)]
    wgu_bf = [A.alloc(NF * 256, BF16) for _ in range(2)]
    sg2 = [A.alloc(256), A.alloc(256)]
    h_sb = A.alloc(8 * 512)
    hsq3 = [A.alloc(512) for _ in range(3)]
    hbf3 = [A.alloc(256, BF16) for _ in range(3)]
    thi = A.alloc(256)
    tlo = A.alloc(256)
    tu1 = A.alloc(128)
    tu2 = A.alloc(128)
    tm2 = A.alloc(128)
    tge = A.alloc(512)
    hsp3 = [A.alloc(256, BF16) for _ in range(3)]
    tpev2 = [A.alloc(256, BF16) for _ in range(2)]
    hv = h_sb[:].rearrange("p (t x) -> p t x", x=512)

    hT_spills = []
    gu_read = {}
    gstag_free = {}
    evac_done = {}
    sparse_done = {}
    hdone = {}
    hsp_free = {}
    hbf_free = {}
    tpev_free = {}
    prev_sq = {}
    sg_read = {}
    sq_guard = {}
    tp_read = {}
    sp_prev = None
    tp_bank = 0
    tp_jobs = []
    sp_jobs = []

    def emit_sparsify_chain(ipg2, tc, t_sq, t_hb, slot):
        nonlocal sp_prev
        hsq = hsq3[slot % 3]
        hbf = hbf3[slot % 3]
        q4 = hsq[:].rearrange("p (g x) -> p g x", x=4)
        q2e = hsq[:].rearrange("p (g x) -> p g x", x=2)[:, :, 0]
        q2o = hsq[:].rearrange("p (g x) -> p g x", x=2)[:, :, 1]
        hi2 = thi[:].rearrange("p (g x) -> p g x", x=2)
        lo2 = tlo[:].rearrange("p (g x) -> p g x", x=2)
        t_1 = E("vector", lambda e, q2e=q2e, q2o=q2o: e.tensor_tensor(
            thi[:], q2e, q2o, ALU.max), [t_sq, sp_prev], "DVE")
        t_2 = E("vector", lambda e, q2e=q2e, q2o=q2o: e.tensor_tensor(
            tlo[:], q2e, q2o, ALU.min), [t_sq], "DVE")
        t_3 = E("vector", lambda e, hi2=hi2: e.tensor_tensor(
            tu1[:], hi2[:, :, 0], hi2[:, :, 1], ALU.min), [t_1], "DVE")
        t_4 = E("vector", lambda e, lo2=lo2: e.tensor_tensor(
            tu2[:], lo2[:, :, 0], lo2[:, :, 1], ALU.max), [t_2], "DVE")
        t_5 = E("vector", lambda e: e.tensor_tensor(tm2[:], tu1[:], tu2[:], ALU.max),
                [t_3, t_4], "DVE")
        m2b = tm2[:].unsqueeze(2).broadcast_to((128, 128, 4))
        t_6 = E("vector", lambda e, q4=q4, m2b=m2b: e.tensor_tensor(
            tge[:].rearrange("p (g x) -> p g x", x=4), q4, m2b, ALU.is_ge),
            [t_5], "DVE")
        hsp = hsp3[slot % 3]
        t_7 = E("vector", lambda e, hbf=hbf, hsp=hsp: e.tensor_tensor(
            hsp[:], hbf[:], tge[:], ALU.mult),
            [t_6] + hsp_free.get(slot % 3, []), "DVE")
        sp_prev = t_7
        prev_sq[slot % 3] = t_6
        hbf_free[slot % 3] = t_7
        tp_jobs.append((ipg2, tc, slot, t_7))

    def emit_transpose(ipg2, tc, slot, t_7):
        nonlocal tp_bank
        hsp = hsp3[slot % 3]
        bank = 6 + tp_bank % 2
        tslot = tp_bank % 2
        tpev = tpev2[tslot]
        tp_bank += 1
        tok = None
        for icl in range(4):
            tok = E("tensor", lambda e, icl=icl, bank=bank, hsp=hsp: e.transpose(
                ps[bank][:].bitcast(BF16)[:, icl * 128:(icl + 1) * 128],
                hsp[:, icl * 128:(icl + 1) * 128], id_sb[:]),
                [t_7, t_id, tp_read.get(bank)],
                "PE" if icl == 3 else None)
        t_tp = tok
        t_te = E("vector", lambda e, bank=bank, tpev=tpev: e.tensor_copy(
            tpev[:], ps[bank][:].bitcast(BF16)[:, 0:512]),
            [t_tp, tpev_free.get(tslot)], "DVE")
        tp_read[bank] = t_te
        hsp_free[slot % 3] = [t_te]
        t_hs = kb.dma("gpsimd",
                      d_hT[:].rearrange("p (c x) -> p c x", x=OWN)
                      [:, ipg2 * 4:(ipg2 + 1) * 4, tc * 128:(tc + 1) * 128],
                      tpev[:].rearrange("p (c x) -> p c x", x=128),
                      "dHT0" if tc % 2 == 0 else "dHT1",
                      [t_te, hT_spills[-2] if len(hT_spills) >= 2 else None])
        tpev_free[tslot] = t_hs
        hT_spills.append(t_hs)

    def drain_one_job():
        if len(tp_jobs) > 1:
            emit_transpose(*tp_jobs.pop(0))

    def wgu_load(ip):
        t_gc = None
        for qt in range(4):
            srcw = wgu[:, ip * 512 + qt * 128:ip * 512 + (qt + 1) * 128]\
                .rearrange("(c p) d -> p c d", p=128)
            qs = qt
            t_gl = kb.dma("sync", gstag2[qs][:].rearrange("p (c d) -> p c d", d=128),
                          srcw, ["dG0", "dG1", "dG2", "dG3"][qs],
                          [gstag_free.get(qs)] +
                          (ph3_tail_wdp + [ph3_tail_pe, ph3_tail_act, ph3_tail_dve]
                           if ip == 0 and qt == 0 else []))
            t_gc = E("vector", lambda e, ip=ip, qt=qt, qs=qs: e.tensor_copy(
                wgu_bf[ip % 2][:].rearrange("p (c d) -> p c d", d=512)
                [:, :, qt * 128:(qt + 1) * 128],
                gstag2[qs][:].rearrange("p (c d) -> p c d", d=128)),
                [t_gl, gu_read.get(ip - 2) if qt == 0 else None,
                 ph3_tail_act if ip < 2 and qt == 0 else None,
                 ph3_tail_ss if ip == 0 and qt == 0 else None, t_gc], "DVE")
            gstag_free[qs] = t_gc
        gu_cast[ip] = t_gc

    gu_cast = {}
    wgu_load(0)
    wgu_load(1)
    spslot = 0
    for ip in range(NIP):
        ip2 = ip % 2
        ipg2 = ip // 2
        t_gc = gu_cast[ip]
        par = ip % 2
        gv = wgu_bf[par][:].rearrange("p (c d) -> p c d", d=512)
        for tc in range(8):
            bank = tc % 6
            tok = None
            for f in range(NF):
                tok = E("tensor", lambda e, f=f, tc=tc, bank=bank, gv=gv: e.matmul(
                    ps[bank][:], xnv[:, f, tc * 128:(tc + 1) * 128],
                    gv[:, f, :], start=(f == 0), stop=(f == NF - 1)),
                    [t_gc, evac_done.get(bank), xn_done[tc // 4]] +
                    ([ph3_tail_pe] if ip == 0 and tc == 0 else []),
                    "PE" if f == NF - 1 else None)
            t_gu = tok
            gu_read[ip] = t_gu
            slot = tc % 2
            t_si = E("scalar", lambda e, bank=bank, slot=slot: e.activation(
                sg2[slot][:], ps[bank][:, 0:256], AF.Silu),
                [t_gu, sg_read.get(slot)], "ACT")
            t_h = E("vector", lambda e, bank=bank, tc=tc, ip2=ip2, slot=slot: e.tensor_tensor(
                hv[:, tc, ip2 * 256:(ip2 + 1) * 256], sg2[slot][:],
                ps[bank][:, 256:512], ALU.mult),
                [t_si, sparse_done.get(tc), sq_guard.get(tc)], "DVE")
            sg_read[slot] = t_h
            evac_done[bank] = t_h
            hdone[tc] = t_h
            if ip2 == 1:
                hsq = hsq3[spslot % 3]
                t_sq = E("scalar", lambda e, tc=tc, hsq=hsq: e.activation(
                    hsq[:], hv[:, tc, :], AF.Square),
                    [t_h, prev_sq.get(spslot % 3)], "ACT")
                hbf = hbf3[spslot % 3]
                t_hb = E("vector", lambda e, tc=tc, hbf=hbf: e.tensor_copy(
                    hbf[:], hv[:, tc, :]),
                    [t_h] + ([hbf_free[spslot % 3]] if spslot % 3 in hbf_free else []),
                    "DVE")
                sparse_done[tc] = t_hb
                sq_guard[tc] = t_sq
                sp_jobs.append((ipg2, tc, t_sq, t_hb, spslot))
                spslot += 1
                if len(sp_jobs) > 2:
                    emit_sparsify_chain(*sp_jobs.pop(0))
                drain_one_job()
            else:
                drain_one_job()
        if ip + 2 < NIP:
            wgu_load(ip + 2)
    while sp_jobs:
        emit_sparsify_chain(*sp_jobs.pop(0))
    while tp_jobs:
        emit_transpose(*tp_jobs.pop(0))
    ph4a_tail = [hT_spills[-1], hT_spills[-2], sp_prev] + ph3_tail_wdp + [
                 sg_read.get(0), sg_read.get(1),
                 tp_read.get(6), tp_read.get(7), gu_read.get(31), dbg_xn_tok]
    A.release()
    A.release()

    A.mark()
    hT_sb = A.alloc(NIC * 256, BF16)
    wd_st2 = [A.alloc(NIC * 64) for _ in range(3)]
    wd_b = [A.alloc(NIC * 64, BF16) for _ in range(2)]
    g2 = [A.alloc(512) for _ in range(2)]
    s_in = [A.alloc(512) for _ in range(2)]
    x_in = [A.alloc(512) for _ in range(2)]
    o_sb = [A.alloc(512) for _ in range(2)]
    hTv = hT_sb[:].rearrange("p (c x) -> p c x", x=512)

    wd_read = {}
    wdst_free = {}
    out_toks = []
    gprev = {}
    cm1prev = {}
    hT_read = []
    for sweep in range(2):
        t_ht1 = kb.dma("sync", hTv[:, 0:NIC // 2, :],
                       d_hT[:].rearrange("p (c x) -> p c x", x=OWN)
                       [:, 0:NIC // 2, sweep * 512:(sweep + 1) * 512],
                       "dHTB", ph4a_tail + hT_read)
        t_ht = kb.dma("sync", hTv[:, NIC // 2:NIC, :],
                      d_hT[:].rearrange("p (c x) -> p c x", x=OWN)
                      [:, NIC // 2:NIC, sweep * 512:(sweep + 1) * 512],
                      "dHTB2" if sweep == 0 else "dHTB3", ph4a_tail)
        hT_read = []
        for fc in range(NF):
            par = fc % 2
            if True:
                casts = []
                for hf in range(2):
                    srcw = wd[hf * (I // 2):(hf + 1) * (I // 2),
                              fc * 128:(fc + 1) * 128].rearrange("(c p) d -> p c d", p=128)
                    hfs = (2 * fc + hf) % 3
                    t_wl = kb.dma("sync",
                                  wd_st2[hfs][:].rearrange("p (c d) -> p c d", d=128),
                                  srcw, ["dWD0", "dWD1", "dWDB0"][hfs],
                                  [wdst_free.get(hfs)] +
                                  (ph4a_tail if sweep == 0 and fc == 0 else []))
                    t_c = E("vector", lambda e, par=par, hf=hf, hfs=hfs: e.tensor_copy(
                        wd_b[par][:].rearrange("p (c d) -> p c d", d=128)
                        [:, hf * (NIC // 2):(hf + 1) * (NIC // 2), :],
                        wd_st2[hfs][:].rearrange("p (c d) -> p c d", d=128)),
                        [t_wl, wd_read.get(fc - 2),
                         ph4a_tail[2] if sweep == 0 and fc < 2 else None], "DVE")
                    wdst_free[hfs] = t_c
                    casts.append(t_c)
                t_wc = casts[-1]
            wdv = wd_b[par][:].rearrange("p (c d) -> p c d", d=128)
            bank = fc % 2
            tok = None
            for ic in range(NIC):
                tok = E("tensor", lambda e, ic=ic, bank=bank, wdv=wdv: e.matmul(
                    ps[bank][:], wdv[:, ic, :], hTv[:, ic, :],
                    start=(ic == 0), stop=(ic == NIC - 1)),
                    [casts[0], t_wc, t_ht1 if ic < NIC // 2 else t_ht,
                     gprev.get(bank)] +
                    (ph4a_tail if sweep == 0 and fc == 0 else []),
                    "PE" if ic == NIC - 1 else None)
            t_dn = tok
            wd_read[fc] = t_dn
            hT_read.append(t_dn)
            buf = fc % 2
            t_g = E("scalar", lambda e, bank=bank, buf=buf: e.activation(
                g2[buf][:], ps[bank][:], AF.Silu),
                [t_dn, cm1prev.get(buf)], "ACT")
            gprev[bank] = t_g
            t_sl = kb.dma("gpsimd", s_in[buf],
                          d_s[fc * 128:(fc + 1) * 128, sweep * 512:(sweep + 1) * 512],
                          "dSI0" if buf == 0 else "dSI1",
                          [s_spills[-1], out_toks[-2] if len(out_toks) >= 2 else None])
            t_xl = kb.dma("gpsimd", x_in[buf],
                          xT[fc * 128:(fc + 1) * 128, sweep * 512:(sweep + 1) * 512],
                          "dXI0" if buf == 0 else "dXI1",
                          [out_toks[-2] if len(out_toks) >= 2 else None])
            t_cm1 = E("vector", lambda e, buf=buf: e.tensor_tensor(
                o_sb[buf], s_in[buf], g2[buf][:], ALU.mult),
                [t_g, t_sl, out_toks[-2] if len(out_toks) >= 2 else None], "DVE")
            cm1prev[buf] = t_cm1
            t_cm2 = E("vector", lambda e, buf=buf: e.tensor_tensor(
                o_sb[buf], o_sb[buf], x_in[buf], ALU.add), [t_cm1, t_xl], "DVE")
            t_o = kb.dma("gpsimd",
                         out[fc * 128:(fc + 1) * 128, sweep * 512:(sweep + 1) * 512],
                         o_sb[buf], "dOUT0" if buf == 0 else "dOUT1", [t_cm2])
            out_toks.append(t_o)
    E("vector", lambda e: e.memset(g2[0][0:1, 0:1], 0.0),
      out_toks + [gprev.get(0), gprev.get(1)], "DVE")
    A.release()

    sems = {}
    for s in kb.sem_names:
        sems[s] = st.enter_context(nc.semaphore(s))
    blk = st.enter_context(nc.Block())
    kb.finalize(blk, sems)
    st.close()
    return nc



def _host_stage(x_full, w_atten, w_proj, w_gate, w_up, w_down):
    wq = w_atten[0:C, :]
    wk = w_atten[C:2 * C, :]
    wv = w_atten[2 * C:3 * C, :]
    wqT = np.ascontiguousarray(wq.T)
    wkT = np.ascontiguousarray(wk.T)
    wvT = np.ascontiguousarray(wv.T)
    wpT = np.ascontiguousarray(w_proj.T)
    wgu = np.empty((C, 2 * I), dtype=np.float32)
    for ip in range(NIP):
        wgu[:, ip * 512:ip * 512 + 256] = w_gate[:, ip * 256:(ip + 1) * 256]
        wgu[:, ip * 512 + 256:(ip + 1) * 512] = w_up[:, ip * 256:(ip + 1) * 256]
    pm = np.zeros((128, 128), dtype=ml_dtypes.bfloat16)
    for i in range(64):
        pm[2 * i + 1, 2 * i] = -1.0
        pm[2 * i, 2 * i + 1] = 1.0
    ident = np.eye(128, dtype=ml_dtypes.bfloat16)
    theta = 1.0 / 10000.0 ** (np.arange(0, D, 2, dtype=np.float64) / D)

    in_maps = []
    for core in range(8):
        b, p = core // 2, core % 2
        blocks = [2 * j + p for j in range(8)] + [2 * j + 1 - p for j in range(8)]
        perm = np.concatenate([np.arange(128 * g, 128 * g + 128) for g in blocks])
        xTc = np.ascontiguousarray(x_full[b].T[:, perm])
        ang = np.outer(theta, perm.astype(np.float64))
        ropeCc = np.repeat(np.cos(ang), 2, axis=0).astype(np.float32)
        ropeSc = np.repeat(np.sin(ang), 2, axis=0).astype(np.float32)
        mk = np.zeros((128, 9 * 512), dtype=np.float32)
        for rel in range(4):
            m = np.zeros((128, 512), dtype=np.float32)
            for u in range(4):
                if rel > u:
                    m[:, u * 128:(u + 1) * 128] = -1e30
                elif rel == u:
                    blkm = np.where(np.tril(np.ones((128, 128), dtype=bool), -1),
                                    np.float32(-1e30), np.float32(0.0))
                    m[:, u * 128:(u + 1) * 128] = blkm
            mk[:, rel * 512:(rel + 1) * 512] = m
        for rel in range(4):
            m = np.zeros((128, 512), dtype=np.float32)
            for u in range(4):
                valid = (rel < u) if p == 0 else (rel <= u)
                if not valid:
                    m[:, u * 128:(u + 1) * 128] = -1e30
            mk[:, (4 + rel) * 512:(5 + rel) * 512] = m
        in_maps.append({
            "xT": xTc,
            "wqT": wqT, "wkT": wkT, "wvT": wvT, "wpT": wpT,
            "wgu": wgu, "wd": np.ascontiguousarray(w_down),
            "ropeC": ropeCc, "ropeS": ropeSc,
            "pmat": pm, "ident": ident,
            "masks": mk.astype(ml_dtypes.bfloat16),
        })
    return in_maps


def kernel(L0_attn_x, L0_attn_w_atten, L0_attn_w_proj,
           L0_ffn_w_gate, L0_ffn_w_up, L0_ffn_w_down):
    x = np.asarray(L0_attn_x, dtype=np.float32)
    in_maps = _host_stage(x, np.asarray(L0_attn_w_atten, np.float32),
                          np.asarray(L0_attn_w_proj, np.float32),
                          np.asarray(L0_ffn_w_gate, np.float32),
                          np.asarray(L0_ffn_w_up, np.float32),
                          np.asarray(L0_ffn_w_down, np.float32))
    nc = build_nc()
    res = run_bass_kernel_spmd(nc, in_maps, core_ids=list(range(8)))
    outf = np.empty((B, T, C), dtype=np.float32)
    for core in range(8):
        b, p = core // 2, core % 2
        oc = np.asarray(res.results[core]["out"])
        for j in range(8):
            gb = 2 * j + p
            outf[b, gb * 128:(gb + 1) * 128, :] = oc[:, j * 128:(j + 1) * 128].T
    return outf


# version 17
# speedup vs baseline: 1.2116x; 1.0113x over previous
import numpy as np
import ml_dtypes
from contextlib import ExitStack

import concourse.bass as bass
import concourse.mybir as mybir
from concourse.bass_utils import run_bass_kernel_spmd

F32 = mybir.dt.float32
BF16 = mybir.dt.bfloat16
ALU = mybir.AluOpType
AF = mybir.ActivationFunctionType

B, T, C, H, D, I = 4, 2048, 2048, 16, 128, 8192
NF = C // 128
NT = T // 512
OWN = T // 2
NG = 8
NIP = I // 256
NIC = I // 128
EXP_BIAS = -2.0
EXP_SCALE = float(1.0 / np.sqrt(128.0))
LN_EPS = 1e-5

ENGINES = ("sync", "tensor", "scalar", "vector", "gpsimd")
DEBUG = False


class KB:

    def __init__(self, nc):
        self.nc = nc
        self.prog = {e: [] for e in ENGINES}
        self.model = {}
        self.watermark = {e: {} for e in ENGINES}
        self.sem_names = []

    def sem(self, name):
        if name not in self.model:
            self.sem_names.append(name)
            self.model[name] = 0
        return name

    def emit(self, eng, fn, waits=(), inc=None, inc_by=1):
        wl = []
        wm = self.watermark[eng]
        coal = {}
        for t in waits:
            if t is None:
                continue
            s, v = t
            coal[s] = max(coal.get(s, 0), v)
        for s, v in coal.items():
            if v > wm.get(s, 0):
                wm[s] = v
                wl.append((s, v))
        tok = None
        if inc is not None:
            self.sem(inc)
            self.model[inc] += inc_by
            tok = (inc, self.model[inc])
        self.prog[eng].append((tuple(wl), fn, (inc, inc_by) if inc else None))
        return tok

    def dma(self, eng, out, in_, sem, waits=()):
        return self.emit(eng, lambda e: e.dma_start(out=out, in_=in_),
                         waits, inc=sem, inc_by=16)

    def finalize(self, block, sems):
        for eng in ENGINES:
            ops = self.prog[eng]
            if not ops:
                continue

            def make(ops):
                def body(handle):
                    for waits, fn, inc in ops:
                        for s, v in waits:
                            handle.wait_ge(sems[s], v)
                        ins = fn(handle)
                        if inc is not None:
                            s, by = inc
                            ins.then_inc(sems[s], by)
                return body
            getattr(block, eng)(make(ops))


class Arena:

    def __init__(self, ap, cols):
        self.ap = ap
        self.cols = cols
        self.ptr = 0
        self.marks = []
        self.peak = 0

    def alloc(self, f32cols, dtype=F32):
        f32cols = int(f32cols)
        a = self.ptr
        assert a + f32cols <= self.cols, f"arena overflow {a}+{f32cols}>{self.cols}"
        self.ptr = a + f32cols
        self.peak = max(self.peak, self.ptr)
        v = self.ap[:, a:a + f32cols]
        if dtype == BF16:
            v = v.bitcast(BF16)
        return v

    def mark(self):
        self.marks.append(self.ptr)

    def release(self):
        self.ptr = self.marks.pop()


def build_nc():
    nc = bass.Bass("TRN2", target_bir_lowering=False)

    xT = nc.dram_tensor("xT", [C, T], F32, kind="ExternalInput").ap()
    wqT = nc.dram_tensor("wqT", [C, C], F32, kind="ExternalInput").ap()
    wkT = nc.dram_tensor("wkT", [C, C], F32, kind="ExternalInput").ap()
    wvT = nc.dram_tensor("wvT", [C, C], F32, kind="ExternalInput").ap()
    wpT = nc.dram_tensor("wpT", [C, C], F32, kind="ExternalInput").ap()
    wgu = nc.dram_tensor("wgu", [C, 2 * I], F32, kind="ExternalInput").ap()
    wd = nc.dram_tensor("wd", [I, C], F32, kind="ExternalInput").ap()
    ropeC = nc.dram_tensor("ropeC", [128, T], F32, kind="ExternalInput").ap()
    ropeS = nc.dram_tensor("ropeS", [128, T], F32, kind="ExternalInput").ap()
    pmat = nc.dram_tensor("pmat", [128, 128], BF16, kind="ExternalInput").ap()
    ident = nc.dram_tensor("ident", [128, 128], BF16, kind="ExternalInput").ap()
    masks = nc.dram_tensor("masks", [128, 8 * 512], BF16, kind="ExternalInput").ap()
    out = nc.dram_tensor("out", [C, OWN], F32, kind="ExternalOutput").ap()

    d_y = nc.dram_tensor("d_y", [C, OWN], BF16).ap()
    d_s = nc.dram_tensor("d_s", [C, OWN], F32).ap()
    d_hT = nc.dram_tensor("d_hT", [128, NIC * OWN], BF16).ap()
    d_wdb = nc.dram_tensor("d_wdb", [128, NIC * C], BF16).ap()
    dbg = {}
    if DEBUG:
        dbg["xn"] = nc.dram_tensor("dbg_xn", [128, NF * T], BF16).ap()
        dbg["k"] = nc.dram_tensor("dbg_k", [128, 2 * T], BF16).ap()
        dbg["q"] = nc.dram_tensor("dbg_q", [128, 2 * OWN], BF16).ap()
        dbg["v"] = nc.dram_tensor("dbg_v", [128, 16 * 256], BF16).ap()
        dbg["hsp"] = nc.dram_tensor("dbg_hsp", [128, 8 * 1024], BF16).ap()

    st = ExitStack()
    ARENA_COLS = 49088
    arena_t = st.enter_context(nc.sbuf_tensor("arena", [128, ARENA_COLS], F32))
    A = Arena(arena_t.ap(), ARENA_COLS)

    rcos = A.alloc(T // 2, BF16)
    rsin = A.alloc(T // 2, BF16)
    pm_sb = A.alloc(64, BF16)
    id_sb = A.alloc(64, BF16)
    mk_sb = A.alloc(8 * 256, BF16)
    ones_colb = A.alloc(1, BF16)
    ones_rowf = A.alloc(128)
    consts = A.alloc(2)
    A.mark()
    xn = A.alloc(NF * T // 2, BF16)

    nc.const_aps.register(F32, EXP_BIAS, consts[:, 0:1])
    nc.const_aps.register(F32, LN_EPS, consts[:, 1:2])

    ps = [st.enter_context(nc.psum_tensor(f"ps{i}", [128, 512], F32)).ap()
          for i in range(8)]

    kb = KB(nc)
    E = kb.emit
    for s in ["dRC", "dRS", "dPM", "dID", "dMK", "dX0", "dX1",
              "dW0", "dW1", "dW2", "dY", "dYB", "dWP0", "dWP1",
              "dG0", "dG1", "dG2", "dG3", "dS", "dHT0", "dHT1", "dHTB", "dHTB2", "dHTB3",
              "dWD0", "dWD1", "dWQ0", "dWQ1", "dWDS0", "dWDS1", "dWDB0", "dWDB1",
              "dSI0", "dSI1", "dXI0", "dXI1", "dOUT0", "dOUT1",
              "PE", "ACT", "DVE", "POOL",
              "dbg0", "dbg1", "dbg2", "dbg3", "dbg4"]:
        kb.sem(s)

    xnv = xn[:].rearrange("p (c x) -> p c x", x=T)

    A.mark()
    cst32 = A.alloc(2 * T)
    t_rc32 = kb.dma("sync", cst32[:, 0:T], ropeC[:], "dRC")
    t_rs32 = kb.dma("sync", cst32[:, T:2 * T], ropeS[:], "dRS")
    t_rcc = E("vector", lambda e: e.tensor_copy(rcos[:], cst32[:, 0:T]), [t_rc32], "DVE")
    t_rsc = E("vector", lambda e: e.tensor_copy(rsin[:], cst32[:, T:2 * T]), [t_rs32], "DVE")
    t_pm = kb.dma("sync", pm_sb[:], pmat[:], "dPM")
    t_id = kb.dma("sync", id_sb[:], ident[:], "dID")
    t_mk = kb.dma("sync", mk_sb[:], masks[:], "dMK")
    t_c0 = E("vector", lambda e: e.memset(consts[:, 0:1], EXP_BIAS), [], "DVE")
    t_c1 = E("vector", lambda e: e.memset(consts[:, 1:2], LN_EPS), [], "DVE")
    t_oc = E("vector", lambda e: e.memset(ones_colb[:, 0:1], 1.0), [], "DVE")
    t_or = E("vector", lambda e: e.memset(ones_rowf[0:1, :], 1.0), [], "DVE")
    A.release()

    A.mark()
    x_f32_2 = [A.alloc(NF * 512) for _ in range(2)]
    x_bf_1 = A.alloc(NF * 256, BF16)
    x_bf_2 = [x_bf_1, x_bf_1]
    xsq_bf = A.alloc(NF * 256, BF16)
    statm2 = [A.alloc(512) for _ in range(2)]
    statv2 = [A.alloc(512) for _ in range(2)]
    mbc = A.alloc(512)
    rbc = A.alloc(512)
    msq = A.alloc(512)

    xqv = xsq_bf[:].rearrange("p (c x) -> p c x", x=512)
    xn_done = []
    prev_reads = [t_rcc, t_rsc]
    prev_sub = {}
    prev_sq = None
    prev_subx = None
    prev_sqx = None
    prev_cb = {}
    prev_mbc = None
    prev_rbc = None
    stat_read = {}
    ps01_read = {}
    prev_sq2 = {}
    prev_mr = None
    stageA = {}

    def ln_stage_a(tt):
        nonlocal prev_sq, prev_sqx
        par = tt % 2
        x_f32 = x_f32_2[par]
        x_bf = x_bf_2[par]
        xbv = x_bf[:].rearrange("p (c x) -> p c x", x=512)
        xfv = x_f32[:].rearrange("p (c x) -> p c x", x=512)
        dsem = "dX0" if tt % 2 == 0 else "dX1"
        srcx = xT[:, tt * 512:(tt + 1) * 512].rearrange("(c p) x -> p c x", p=128)
        t_x = kb.dma("sync", xfv, srcx, prev_reads + [prev_sub.get(par)] and dsem, []) \
            if False else kb.dma("sync", xfv, srcx, dsem,
                                 prev_reads + [prev_sub.get(par)])
        t_cb = E("vector", lambda e, x_bf=x_bf, x_f32=x_f32: e.tensor_copy(
            x_bf[:], x_f32[:]),
            [t_x, prev_cb.get(0), prev_sq, prev_subx, prev_sqx], "DVE")
        prev_cb[0] = t_cb
        t_sq = E("scalar", lambda e, x_bf=x_bf: e.activation(xsq_bf[:], x_bf[:], AF.Square),
                 [t_cb, prev_sq2.get(0), prev_sub.get(0), prev_sub.get(1)], "ACT")
        prev_sqx = t_sq
        tok = None
        for c in range(NF):
            tok = E("tensor", lambda e, c=c, xbv=xbv: e.matmul(
                ps[0][0:1, :], ones_colb[:, 0:1], xbv[:, c, :],
                start=(c == 0), stop=(c == NF - 1)),
                [t_cb, t_oc, ps01_read.get(0)] + prev_reads,
                "PE" if c == NF - 1 else None)
        t_s1 = tok
        for c in range(NF):
            tok = E("tensor", lambda e, c=c: e.matmul(
                ps[1][0:1, :], ones_colb[:, 0:1], xqv[:, c, :],
                start=(c == 0), stop=(c == NF - 1)),
                [t_sq, t_oc, ps01_read.get(1)], "PE" if c == NF - 1 else None)
        t_s2 = tok
        t_m = E("vector", lambda e, par=par: e.tensor_scalar_mul(
            statm2[par][0:1, :], ps[0][0:1, :], 1.0 / C),
            [t_s1, stat_read.get(par)], "DVE")
        t_v0 = E("vector", lambda e, par=par: e.tensor_scalar_mul(
            statv2[par][0:1, :], ps[1][0:1, :], 1.0 / C),
            [t_s2, stat_read.get(par)], "DVE")
        ps01_read[0] = t_m
        ps01_read[1] = t_v0
        prev_sq2[0] = t_s2
        stageA[tt] = (t_m, t_v0, t_s2, t_cb, xbv, xfv)

    def ln_stage_b(tt):
        nonlocal prev_sq, prev_mbc, prev_rbc, prev_subx, prev_mr
        t_m, t_v0, t_s2, t_cb, xbv, xfv = stageA[tt]
        par = tt % 2
        t_ms = E("vector", lambda e, par=par: e.tensor_tensor(
            msq[0:1, :], statm2[par][0:1, :], statm2[par][0:1, :],
            ALU.mult), [t_m], "DVE")
        t_v = E("vector", lambda e, par=par: e.tensor_tensor(
            statv2[par][0:1, :], statv2[par][0:1, :], msq[0:1, :],
            ALU.subtract), [t_v0, t_ms], "DVE")
        t_ln = E("scalar", lambda e, par=par: e.activation(
            statv2[par][0:1, :], statv2[par][0:1, :], AF.Ln,
            bias=LN_EPS), [t_v, t_c1], "ACT")
        t_rst = E("scalar", lambda e, par=par: e.activation(
            statv2[par][0:1, :], statv2[par][0:1, :], AF.Exp,
            scale=-0.5), [t_ln], "ACT")
        t_mr = E("vector", lambda e, par=par: e.tensor_tensor(
            msq[0:1, :], statm2[par][0:1, :], statv2[par][0:1, :], ALU.mult),
            [t_m, t_rst, prev_mr], "DVE")
        t_mb = E("tensor", lambda e: e.matmul(
            ps[2][:], ones_rowf[0:1, :], msq[0:1, :],
            start=True, stop=True), [t_mr, t_or, prev_mbc], "PE")
        t_mbc = E("vector", lambda e: e.tensor_copy(mbc[:], ps[2][:]), [t_mb], "DVE")
        t_rb = E("tensor", lambda e, par=par: e.matmul(
            ps[3][:], ones_rowf[0:1, :], statv2[par][0:1, :],
            start=True, stop=True), [t_rst, t_or, prev_rbc], "PE")
        t_rbc = E("vector", lambda e: e.tensor_copy(rbc[:], ps[3][:]), [t_rb], "DVE")
        mb3 = mbc[:].unsqueeze(1).broadcast_to((128, NF, 512))
        rb3 = rbc[:].unsqueeze(1).broadcast_to((128, NF, 512))
        t_sub = E("vector", lambda e, xfv=xfv, rb3=rb3: e.tensor_tensor(
            xsq_bf[:].rearrange("p (c x) -> p c x", x=512), xfv, rb3, ALU.mult),
            [t_rbc, prev_sq2.get(0), prev_sqx], "DVE")
        prev_subx = t_sub
        t_xn = E("vector", lambda e, tt=tt, mb3=mb3: e.tensor_tensor(
            xnv[:, :, tt * 512:(tt + 1) * 512],
            xsq_bf[:].rearrange("p (c x) -> p c x", x=512), mb3, ALU.subtract),
            [t_sub, t_mbc], "DVE")
        xn_done.append(t_xn)
        prev_sub[par] = t_xn
        prev_mbc = t_mbc
        prev_rbc = t_rbc
        prev_mr = t_mb
        stat_read[par] = t_rb

    ln_stage_a(0)
    prev_reads = []
    for tt in range(1, NT):
        ln_stage_a(tt)
        ln_stage_b(tt - 1)
    ln_stage_b(NT - 1)
    A.release()
    dbg_xn_tok = None
    if DEBUG:
        dbg_xn_tok = kb.dma("gpsimd", dbg["xn"][:], xn[:], "dbg0", xn_done)

    A.mark()
    wstag2 = [A.alloc(NF * 256) for _ in range(2)]
    wq_bf = A.alloc(NF * 128, BF16)
    wk_bf = A.alloc(NF * 128, BF16)
    wv_bf = A.alloc(NF * 128, BF16)
    kh = A.alloc(T, BF16)
    vg = A.alloc(16 * 128, BF16)
    qh = A.alloc(OWN, BF16)
    rt1 = A.alloc(512)
    rt2 = A.alloc(512)
    kraw = A.alloc(256, BF16)
    krawb = A.alloc(256, BF16)
    pT = [A.alloc(256, BF16) for _ in range(3)]
    recip = A.alloc(512)
    rep_sb = A.alloc(512)
    y_sb = A.alloc(256, BF16)

    wqv = wq_bf[:].rearrange("p (c d) -> p c d", d=256)
    wkv = wk_bf[:].rearrange("p (c d) -> p c d", d=256)
    wvv = wv_bf[:].rearrange("p (c d) -> p c d", d=256)

    y_spills = []
    wfree = {"q": None, "k": None, "v": None}
    stag_free = {}
    kv_read_tok = None
    st_free = {}
    pT_free = {}
    prev_y = None
    prev_rcp = None
    prev_rpe = None
    last_exp = None
    t_wc_last = None
    dbg_war = []

    for g in range(NG):
        toks_w = {}
        for wi, (wap, wbf, nm, sem) in enumerate(
                ((wkT, wk_bf, "k", "dW0"), (wvT, wv_bf, "v", "dW1"),
                 (wqT, wq_bf, "q", "dW2"))):
            srcw = wap[:, g * 256:(g + 1) * 256].rearrange("(c p) d -> p c d", p=128)
            sl = (3 * g + wi) % 2
            t_l = kb.dma("sync", wstag2[sl][:].rearrange("p (c d) -> p c d", d=256),
                         srcw, sem,
                         [stag_free.get(sl)] + (xn_done if g == 0 else []))
            t_c = E("vector", lambda e, wbf=wbf, sl=sl: e.tensor_copy(
                wbf[:], wstag2[sl][:]), [t_l, wfree[nm]], "DVE")
            stag_free[sl] = t_c
            toks_w[nm] = t_c
        t_wk, t_wv, t_wq = toks_w["k"], toks_w["v"], toks_w["q"]

        units = [("k", h2, tile) for h2 in range(2) for tile in range(NT)] + \
                [("q", h2, tile) for h2 in range(2) for tile in range(2)]
        rope_done = []
        evac_free = {}
        sw_free = None
        prev_rt = None
        kraw2 = [kraw, krawb]
        kraw_readers = {0: [], 1: []}
        pend = None
        for ui, (kind, h2, tile) in enumerate(units):
            bank = 4 + ui % 2
            wv_ = wkv if kind == "k" else wqv
            tok = None
            for f in range(NF):
                tok = E("tensor", lambda e, f=f, h2=h2, bank=bank, tile=tile, wv_=wv_: e.matmul(
                    ps[bank][:], wv_[:, f, h2 * 128:(h2 + 1) * 128],
                    xnv[:, f, tile * 512:(tile + 1) * 512],
                    start=(f == 0), stop=(f == NF - 1)),
                    [t_wk if kind == "k" else t_wq, xn_done[tile],
                     evac_free.get(bank), kv_read_tok],
                    "PE" if f == NF - 1 else None)
            t_p = tok
            if kind == "k":
                k_last_mm = t_p
            else:
                q_last_mm = t_p
            slot = ui % 2
            t_r = E("scalar", lambda e, bank=bank, slot=slot: e.copy(
                kraw2[slot][:], ps[bank][:]),
                [t_p] + kraw_readers[slot], "ACT")
            evac_free[bank] = t_r

            def emit_rope(uidx, t_evac):
                nonlocal sw_free, prev_rt
                kind_, h2_, tile_ = units[uidx]
                slot_ = uidx % 2
                t_sw = E("tensor", lambda e, slot_=slot_: e.matmul(
                    ps[6][:], pm_sb[:], kraw2[slot_][:], start=True, stop=True),
                    [t_evac, t_pm, sw_free], "PE")
                t_m1 = E("vector", lambda e, tile_=tile_, slot_=slot_: e.tensor_tensor(
                    rt1[:], kraw2[slot_][:], rcos[:, tile_ * 512:(tile_ + 1) * 512],
                    ALU.mult), [t_evac, t_rcc, prev_rt], "DVE")
                t_m2 = E("vector", lambda e, tile_=tile_: e.tensor_tensor(
                    rt2[:], ps[6][:], rsin[:, tile_ * 512:(tile_ + 1) * 512], ALU.mult),
                    [t_sw, t_rsc, prev_rt], "DVE")
                sw_free = t_m2
                if kind_ == "k":
                    dst = kh[:, h2_ * T + tile_ * 512:h2_ * T + (tile_ + 1) * 512]
                else:
                    dst = qh[:, h2_ * OWN + tile_ * 512:h2_ * OWN + (tile_ + 1) * 512]
                t_ro = E("vector", lambda e, dst=dst: e.tensor_tensor(
                    dst, rt1[:], rt2[:], ALU.add),
                    [t_m1, t_m2, kv_read_tok] + dbg_war, "DVE")
                rope_done.append(t_ro)
                prev_rt = t_ro
                kraw_readers[slot_] = [t_sw, t_m1]

            if pend is not None:
                emit_rope(*pend)
            pend = (ui, t_r)
        emit_rope(*pend)
        rope_all = rope_done[-1]

        v_done = []
        for tc in range(16):
            bank = 4 + tc % 2
            tok = None
            for f in range(NF):
                tok = E("tensor", lambda e, f=f, tc=tc, bank=bank: e.matmul(
                    ps[bank][:, 0:256],
                    xnv[:, f, tc * 128:(tc + 1) * 128],
                    wvv[:, f, :],
                    start=(f == 0), stop=(f == NF - 1)),
                    [t_wv, xn_done[tc // 4], evac_free.get(bank), kv_read_tok],
                    "PE" if f == NF - 1 else None)
            t_vp = tok
            v_last_mm = t_vp
            t_ve = E("scalar", lambda e, tc=tc, bank=bank: e.copy(
                vg[:, tc * 256:(tc + 1) * 256], ps[bank][:, 0:256]),
                [t_vp] + kraw_readers[0] + kraw_readers[1] + dbg_war, "ACT")
            evac_free[bank] = t_ve
            v_done.append(t_ve)
        v_all = v_done[-1]

        if DEBUG and g == 0:
            dbg_war.append(kb.dma("gpsimd", dbg["k"][:], kh[:], "dbg1", [rope_all]))
            dbg_war.append(kb.dma("gpsimd", dbg["q"][:], qh[:], "dbg2", [rope_all]))
            dbg_war.append(kb.dma("gpsimd", dbg["v"][:], vg[:], "dbg3", [v_all]))

        group_pe_tail = None
        for h2 in range(2):
            head = 2 * g + h2
            for a in range(2):
                L = list(range(0, 4 * a + 4)) + list(range(8, 8 + 4 * a + 4))
                nL = len(L)
                den_last = None
                pv_last = None
                exps = [None] * nL

                def emit_smask(qi):
                    qpos = L[qi]
                    sbank = qi % 3
                    slot = qi % 3
                    if qpos < 8:
                        rel = qpos - 4 * a
                        mi = rel if 0 <= rel <= 3 else 8
                    else:
                        rel = qpos - 8 - 4 * a
                        mi = 4 + rel if 0 <= rel <= 3 else 8
                    t_s = E("tensor", lambda e, qpos=qpos, h2=h2, a=a, sbank=sbank,
                            mi=mi: e.matmul(
                        ps[[0, 1, 4][sbank]][:],
                        kh[:, h2 * T + qpos * 128:h2 * T + (qpos + 1) * 128],
                        qh[:, h2 * OWN + a * 512:h2 * OWN + (a + 1) * 512],
                        start=True, stop=(mi == 8)),
                        [rope_all, st_free.get(sbank), v_all], "PE")
                    t_mm = t_s if mi == 8 else E(
                        "tensor", lambda e, mi=mi, sbank=sbank: e.matmul(
                            ps[[0, 1, 4][sbank]][:], id_sb[:],
                            mk_sb[:, mi * 512:(mi + 1) * 512],
                            start=False, stop=True),
                        [t_s, t_id, t_mk], "PE")
                    t_e = E("scalar", lambda e, sbank=sbank, slot=slot: e.activation(
                        pT[slot][:], ps[[0, 1, 4][sbank]][:], AF.Exp,
                        bias=EXP_BIAS, scale=EXP_SCALE),
                        [t_mm, t_c0, pT_free.get(slot)], "ACT")
                    st_free[sbank] = t_e
                    exps[qi] = t_e

                def emit_denpv(qi):
                    nonlocal den_last, pv_last, prev_rcp, prev_rpe, prev_y
                    qpos = L[qi]
                    slot = qi % 3
                    t_e = exps[qi]
                    t_d = E("tensor", lambda e, slot=slot, first=(qi == 0),
                            last=(qi == nL - 1): e.matmul(
                        ps[3][0:1, :], ones_colb[:, 0:1], pT[slot][:],
                        start=first, stop=last),
                        [t_e, t_oc] + ([prev_rcp, prev_rpe] if qi == 0 else []), "PE")
                    den_last = t_d
                    t_pv = E("tensor", lambda e, qpos=qpos, h2=h2, slot=slot,
                             first=(qi == 0), last=(qi == nL - 1): e.matmul(
                        ps[2][:], vg[:, qpos * 256 + h2 * 128:qpos * 256 + (h2 + 1) * 128],
                        pT[slot][:], start=first, stop=last),
                        [t_e, v_all] + ([prev_y] if qi == 0 else []), "PE")
                    pT_free[slot] = t_pv
                    pv_last = t_pv

                for qi in range(nL):
                    emit_smask(qi)
                    if qi >= 1:
                        emit_denpv(qi - 1)
                emit_denpv(nL - 1)
                t_rcp = E("vector", lambda e: e.reciprocal(recip[0:1, :], ps[3][0:1, :]),
                          [den_last, prev_rpe], "DVE")
                prev_rcp = t_rcp
                t_rp = E("tensor", lambda e: e.matmul(ps[3][:], ones_rowf[0:1, :],
                                                      recip[0:1, :], start=True, stop=True),
                        [t_rcp, t_or], "PE")
                t_rpe = E("scalar", lambda e: e.copy(rep_sb[:], ps[3][:]),
                          [t_rp, prev_y], "ACT")
                prev_rpe = t_rpe
                t_y = E("vector", lambda e: e.tensor_tensor(y_sb[:], ps[2][:], rep_sb[:],
                                                            ALU.mult),
                        [t_rpe, pv_last, y_spills[-1] if y_spills else None], "DVE")
                prev_y = t_y
                t_ysp = kb.dma("gpsimd",
                               d_y[head * 128:(head + 1) * 128, a * 512:(a + 1) * 512],
                               y_sb[:], "dY", [t_y])
                y_spills.append(t_ysp)
                group_pe_tail = t_rp
        kv_read_tok = group_pe_tail
        wfree = {"q": q_last_mm, "k": k_last_mm, "v": v_last_mm}

    ph2_tail_pe = kv_read_tok
    ph2_tail_dve = prev_y
    ph2_tail_act = prev_rpe
    A.release()

    A.mark()
    y_all = A.alloc(NF * OWN // 2, BF16)
    wstag3 = A.alloc(NF * 512)
    wp_bf = [A.alloc(NF * 256, BF16) for _ in range(2)]
    s_stage = A.alloc(512)
    yav = y_all[:].rearrange("p (c x) -> p c x", x=OWN)
    wdp_last = []
    t_yb = kb.dma("sync", yav, d_y[:, :].rearrange("(c p) x -> p c x", p=128), "dYB",
                  y_spills + wdp_last + [ph2_tail_pe, ph2_tail_dve, ph2_tail_act,
                                         stag_free.get(0), stag_free.get(1)])
    s_spills = []
    wp_read = {}
    stag3_free = None
    pj_prev = []
    prev_se = None
    for cs4 in range(4):
        par = cs4 % 2
        src = wpT[:, cs4 * 512:(cs4 + 1) * 512].rearrange("(c p) d -> p c d", p=128)
        t_wl = kb.dma("sync", wstag3[:].rearrange("p (c d) -> p c d", d=512), src,
                      "dWP0" if par == 0 else "dWP1",
                      [stag3_free] + ([ph2_tail_pe, ph2_tail_dve, ph2_tail_act,
                                       stag_free.get(0), stag_free.get(1)] + wdp_last
                                      if cs4 == 0 else []))
        t_wc = E("vector", lambda e, par=par: e.tensor_copy(wp_bf[par][:], wstag3[:]),
                 [t_wl, wp_read.get(cs4 - 2), ph2_tail_act if cs4 < 2 else None] +
                 (wdp_last if cs4 == 0 else []), "DVE")
        stag3_free = t_wc
        wpv = wp_bf[par][:].rearrange("p (c d) -> p c d", d=512)
        for cc in range(4):
            c_chunk = cs4 * 4 + cc
            for tt2 in range(2):
                bank = (cc * 2 + tt2) % 4
                tok = None
                for yc in range(NF):
                    tok = E("tensor", lambda e, yc=yc, cc=cc, tt2=tt2, bank=bank,
                            wpv=wpv: e.matmul(
                        ps[bank][:], wpv[:, yc, cc * 128:(cc + 1) * 128],
                        yav[:, yc, tt2 * 512:(tt2 + 1) * 512],
                        start=(yc == 0), stop=(yc == NF - 1)),
                        [t_wc, t_yb] + pj_prev[-4:], "PE" if yc == NF - 1 else None)
                t_pj = tok
                wp_read[cs4] = t_pj
                t_se = E("scalar", lambda e, bank=bank: e.copy(s_stage[:], ps[bank][:]),
                         [t_pj, s_spills[-1] if s_spills else None], "ACT")
                pj_prev.append(t_se)
                prev_se = t_se
                t_ss = kb.dma("gpsimd",
                              d_s[c_chunk * 128:(c_chunk + 1) * 128,
                                  tt2 * 512:(tt2 + 1) * 512],
                              s_stage[:], "dS", [t_se])
                s_spills.append(t_ss)
    ph3_tail_pe = wp_read[3]
    ph3_tail_act = prev_se
    ph3_tail_dve = stag3_free
    ph3_tail_ss = s_spills[-1]
    ph3_tail_wdp = wdp_last
    A.release()

    A.mark()
    gstag2 = [A.alloc(NF * 128) for _ in range(4)]
    wgu_bf = [A.alloc(NF * 256, BF16) for _ in range(2)]
    sg2 = [A.alloc(256), A.alloc(256)]
    h_sb = A.alloc(8 * 512)
    hsq3 = [A.alloc(512) for _ in range(3)]
    hbf3 = [A.alloc(256, BF16) for _ in range(3)]
    thi = A.alloc(256)
    tlo = A.alloc(256)
    tu1 = A.alloc(128)
    tu2 = A.alloc(128)
    tm2 = A.alloc(128)
    tge = A.alloc(512)
    hsp3 = [A.alloc(256, BF16) for _ in range(3)]
    tpev2 = [A.alloc(256, BF16) for _ in range(2)]
    hv = h_sb[:].rearrange("p (t x) -> p t x", x=512)

    hT_spills = []
    gu_read = {}
    gstag_free = {}
    evac_done = {}
    sparse_done = {}
    hdone = {}
    hsp_free = {}
    hbf_free = {}
    tpev_free = {}
    prev_sq = {}
    sg_read = {}
    sq_guard = {}
    tp_read = {}
    sp_prev = None
    tp_bank = 0
    tp_jobs = []
    sp_jobs = []

    def emit_sparsify_chain(ipg2, tc, t_sq, t_hb, slot):
        nonlocal sp_prev
        hsq = hsq3[slot % 3]
        hbf = hbf3[slot % 3]
        q4 = hsq[:].rearrange("p (g x) -> p g x", x=4)
        q2e = hsq[:].rearrange("p (g x) -> p g x", x=2)[:, :, 0]
        q2o = hsq[:].rearrange("p (g x) -> p g x", x=2)[:, :, 1]
        hi2 = thi[:].rearrange("p (g x) -> p g x", x=2)
        lo2 = tlo[:].rearrange("p (g x) -> p g x", x=2)
        t_1 = E("vector", lambda e, q2e=q2e, q2o=q2o: e.tensor_tensor(
            thi[:], q2e, q2o, ALU.max), [t_sq, sp_prev], "DVE")
        t_2 = E("vector", lambda e, q2e=q2e, q2o=q2o: e.tensor_tensor(
            tlo[:], q2e, q2o, ALU.min), [t_sq], "DVE")
        t_3 = E("vector", lambda e, hi2=hi2: e.tensor_tensor(
            tu1[:], hi2[:, :, 0], hi2[:, :, 1], ALU.min), [t_1], "DVE")
        t_4 = E("vector", lambda e, lo2=lo2: e.tensor_tensor(
            tu2[:], lo2[:, :, 0], lo2[:, :, 1], ALU.max), [t_2], "DVE")
        t_5 = E("vector", lambda e: e.tensor_tensor(tm2[:], tu1[:], tu2[:], ALU.max),
                [t_3, t_4], "DVE")
        m2b = tm2[:].unsqueeze(2).broadcast_to((128, 128, 4))
        t_6 = E("vector", lambda e, q4=q4, m2b=m2b: e.tensor_tensor(
            tge[:].rearrange("p (g x) -> p g x", x=4), q4, m2b, ALU.is_ge),
            [t_5], "DVE")
        hsp = hsp3[slot % 3]
        t_7 = E("vector", lambda e, hbf=hbf, hsp=hsp: e.tensor_tensor(
            hsp[:], hbf[:], tge[:], ALU.mult),
            [t_6] + hsp_free.get(slot % 3, []), "DVE")
        sp_prev = t_7
        prev_sq[slot % 3] = t_6
        hbf_free[slot % 3] = t_7
        tp_jobs.append((ipg2, tc, slot, t_7))

    def emit_transpose(ipg2, tc, slot, t_7):
        nonlocal tp_bank
        hsp = hsp3[slot % 3]
        bank = 6 + tp_bank % 2
        tslot = tp_bank % 2
        tpev = tpev2[tslot]
        tp_bank += 1
        tok = None
        for icl in range(4):
            tok = E("tensor", lambda e, icl=icl, bank=bank, hsp=hsp: e.transpose(
                ps[bank][:].bitcast(BF16)[:, icl * 128:(icl + 1) * 128],
                hsp[:, icl * 128:(icl + 1) * 128], id_sb[:]),
                [t_7, t_id, tp_read.get(bank)],
                "PE" if icl == 3 else None)
        t_tp = tok
        t_te = E("vector", lambda e, bank=bank, tpev=tpev: e.tensor_copy(
            tpev[:], ps[bank][:].bitcast(BF16)[:, 0:512]),
            [t_tp, tpev_free.get(tslot)], "DVE")
        tp_read[bank] = t_te
        hsp_free[slot % 3] = [t_te]
        t_hs = kb.dma("gpsimd",
                      d_hT[:].rearrange("p (c x) -> p c x", x=OWN)
                      [:, ipg2 * 4:(ipg2 + 1) * 4, tc * 128:(tc + 1) * 128],
                      tpev[:].rearrange("p (c x) -> p c x", x=128),
                      "dHT0" if tc % 2 == 0 else "dHT1",
                      [t_te, hT_spills[-2] if len(hT_spills) >= 2 else None])
        tpev_free[tslot] = t_hs
        hT_spills.append(t_hs)

    def drain_one_job():
        if len(tp_jobs) > 1:
            emit_transpose(*tp_jobs.pop(0))

    def wgu_load(ip):
        t_gc = None
        for qt in range(4):
            srcw = wgu[:, ip * 512 + qt * 128:ip * 512 + (qt + 1) * 128]\
                .rearrange("(c p) d -> p c d", p=128)
            qs = qt
            t_gl = kb.dma("sync", gstag2[qs][:].rearrange("p (c d) -> p c d", d=128),
                          srcw, ["dG0", "dG1", "dG2", "dG3"][qs],
                          [gstag_free.get(qs)] +
                          (ph3_tail_wdp + [ph3_tail_pe, ph3_tail_act, ph3_tail_dve]
                           if ip == 0 and qt == 0 else []))
            t_gc = E("vector", lambda e, ip=ip, qt=qt, qs=qs: e.tensor_copy(
                wgu_bf[ip % 2][:].rearrange("p (c d) -> p c d", d=512)
                [:, :, qt * 128:(qt + 1) * 128],
                gstag2[qs][:].rearrange("p (c d) -> p c d", d=128)),
                [t_gl, gu_read.get(ip - 2) if qt == 0 else None,
                 ph3_tail_act if ip < 2 and qt == 0 else None,
                 ph3_tail_ss if ip == 0 and qt == 0 else None, t_gc], "DVE")
            gstag_free[qs] = t_gc
        gu_cast[ip] = t_gc

    gu_cast = {}
    wgu_load(0)
    wgu_load(1)
    spslot = 0
    for ip in range(NIP):
        ip2 = ip % 2
        ipg2 = ip // 2
        t_gc = gu_cast[ip]
        par = ip % 2
        gv = wgu_bf[par][:].rearrange("p (c d) -> p c d", d=512)
        for tc in range(8):
            bank = tc % 6
            tok = None
            for f in range(NF):
                tok = E("tensor", lambda e, f=f, tc=tc, bank=bank, gv=gv: e.matmul(
                    ps[bank][:], xnv[:, f, tc * 128:(tc + 1) * 128],
                    gv[:, f, :], start=(f == 0), stop=(f == NF - 1)),
                    [t_gc, evac_done.get(bank), xn_done[tc // 4]] +
                    ([ph3_tail_pe] if ip == 0 and tc == 0 else []),
                    "PE" if f == NF - 1 else None)
            t_gu = tok
            gu_read[ip] = t_gu
            slot = tc % 2
            t_si = E("scalar", lambda e, bank=bank, slot=slot: e.activation(
                sg2[slot][:], ps[bank][:, 0:256], AF.Silu),
                [t_gu, sg_read.get(slot)], "ACT")
            t_h = E("vector", lambda e, bank=bank, tc=tc, ip2=ip2, slot=slot: e.tensor_tensor(
                hv[:, tc, ip2 * 256:(ip2 + 1) * 256], sg2[slot][:],
                ps[bank][:, 256:512], ALU.mult),
                [t_si, sparse_done.get(tc), sq_guard.get(tc)], "DVE")
            sg_read[slot] = t_h
            evac_done[bank] = t_h
            hdone[tc] = t_h
            if ip2 == 1:
                hsq = hsq3[spslot % 3]
                t_sq = E("scalar", lambda e, tc=tc, hsq=hsq: e.activation(
                    hsq[:], hv[:, tc, :], AF.Square),
                    [t_h, prev_sq.get(spslot % 3)], "ACT")
                hbf = hbf3[spslot % 3]
                t_hb = E("vector", lambda e, tc=tc, hbf=hbf: e.tensor_copy(
                    hbf[:], hv[:, tc, :]),
                    [t_h] + ([hbf_free[spslot % 3]] if spslot % 3 in hbf_free else []),
                    "DVE")
                sparse_done[tc] = t_hb
                sq_guard[tc] = t_sq
                sp_jobs.append((ipg2, tc, t_sq, t_hb, spslot))
                spslot += 1
                if len(sp_jobs) > 2:
                    emit_sparsify_chain(*sp_jobs.pop(0))
                drain_one_job()
            else:
                drain_one_job()
        if ip + 2 < NIP:
            wgu_load(ip + 2)
    while sp_jobs:
        emit_sparsify_chain(*sp_jobs.pop(0))
    while tp_jobs:
        emit_transpose(*tp_jobs.pop(0))
    ph4a_tail = [hT_spills[-1], hT_spills[-2], sp_prev] + ph3_tail_wdp + [
                 sg_read.get(0), sg_read.get(1),
                 tp_read.get(6), tp_read.get(7), gu_read.get(31), dbg_xn_tok]
    A.release()
    A.release()

    A.mark()
    hT_sb = A.alloc(NIC * 256, BF16)
    wd_st2 = [A.alloc(NIC * 64) for _ in range(3)]
    wd_b = [A.alloc(NIC * 64, BF16) for _ in range(2)]
    g2 = [A.alloc(512) for _ in range(2)]
    s_in = [A.alloc(512) for _ in range(2)]
    x_in = [A.alloc(512) for _ in range(2)]
    o_sb = [A.alloc(512) for _ in range(2)]
    hTv = hT_sb[:].rearrange("p (c x) -> p c x", x=512)

    wd_read = {}
    wdst_free = {}
    out_toks = []
    gprev = {}
    cm1prev = {}
    hT_read = []
    for sweep in range(2):
        t_ht1 = kb.dma("sync", hTv[:, 0:NIC // 2, :],
                       d_hT[:].rearrange("p (c x) -> p c x", x=OWN)
                       [:, 0:NIC // 2, sweep * 512:(sweep + 1) * 512],
                       "dHTB", ph4a_tail + hT_read)
        t_ht = kb.dma("sync", hTv[:, NIC // 2:NIC, :],
                      d_hT[:].rearrange("p (c x) -> p c x", x=OWN)
                      [:, NIC // 2:NIC, sweep * 512:(sweep + 1) * 512],
                      "dHTB2" if sweep == 0 else "dHTB3", ph4a_tail)
        hT_read = []
        for fc in range(NF):
            par = fc % 2
            if True:
                casts = []
                for hf in range(2):
                    srcw = wd[hf * (I // 2):(hf + 1) * (I // 2),
                              fc * 128:(fc + 1) * 128].rearrange("(c p) d -> p c d", p=128)
                    hfs = (2 * fc + hf) % 3
                    t_wl = kb.dma("sync",
                                  wd_st2[hfs][:].rearrange("p (c d) -> p c d", d=128),
                                  srcw, ["dWD0", "dWD1", "dWDB0"][hfs],
                                  [wdst_free.get(hfs)] +
                                  (ph4a_tail if sweep == 0 and fc == 0 else []))
                    t_c = E("vector", lambda e, par=par, hf=hf, hfs=hfs: e.tensor_copy(
                        wd_b[par][:].rearrange("p (c d) -> p c d", d=128)
                        [:, hf * (NIC // 2):(hf + 1) * (NIC // 2), :],
                        wd_st2[hfs][:].rearrange("p (c d) -> p c d", d=128)),
                        [t_wl, wd_read.get(fc - 2),
                         ph4a_tail[2] if sweep == 0 and fc < 2 else None], "DVE")
                    wdst_free[hfs] = t_c
                    casts.append(t_c)
                t_wc = casts[-1]
            wdv = wd_b[par][:].rearrange("p (c d) -> p c d", d=128)
            bank = fc % 2
            tok = None
            for ic in range(NIC):
                tok = E("tensor", lambda e, ic=ic, bank=bank, wdv=wdv: e.matmul(
                    ps[bank][:], wdv[:, ic, :], hTv[:, ic, :],
                    start=(ic == 0), stop=(ic == NIC - 1)),
                    [casts[0], t_wc, t_ht1 if ic < NIC // 2 else t_ht,
                     gprev.get(bank)] +
                    (ph4a_tail if sweep == 0 and fc == 0 else []),
                    "PE" if ic == NIC - 1 else None)
            t_dn = tok
            wd_read[fc] = t_dn
            hT_read.append(t_dn)
            buf = fc % 2
            t_g = E("scalar", lambda e, bank=bank, buf=buf: e.activation(
                g2[buf][:], ps[bank][:], AF.Silu),
                [t_dn, cm1prev.get(buf)], "ACT")
            gprev[bank] = t_g
            t_sl = kb.dma("gpsimd", s_in[buf],
                          d_s[fc * 128:(fc + 1) * 128, sweep * 512:(sweep + 1) * 512],
                          "dSI0" if buf == 0 else "dSI1",
                          [s_spills[-1], out_toks[-2] if len(out_toks) >= 2 else None])
            t_xl = kb.dma("gpsimd", x_in[buf],
                          xT[fc * 128:(fc + 1) * 128, sweep * 512:(sweep + 1) * 512],
                          "dXI0" if buf == 0 else "dXI1",
                          [out_toks[-2] if len(out_toks) >= 2 else None])
            t_cm1 = E("vector", lambda e, buf=buf: e.tensor_tensor(
                o_sb[buf], s_in[buf], g2[buf][:], ALU.mult),
                [t_g, t_sl, out_toks[-2] if len(out_toks) >= 2 else None], "DVE")
            cm1prev[buf] = t_cm1
            t_cm2 = E("vector", lambda e, buf=buf: e.tensor_tensor(
                o_sb[buf], o_sb[buf], x_in[buf], ALU.add), [t_cm1, t_xl], "DVE")
            t_o = kb.dma("gpsimd",
                         out[fc * 128:(fc + 1) * 128, sweep * 512:(sweep + 1) * 512],
                         o_sb[buf], "dOUT0" if buf == 0 else "dOUT1", [t_cm2])
            out_toks.append(t_o)
    E("vector", lambda e: e.memset(g2[0][0:1, 0:1], 0.0),
      out_toks + [gprev.get(0), gprev.get(1)], "DVE")
    A.release()

    sems = {}
    for s in kb.sem_names:
        sems[s] = st.enter_context(nc.semaphore(s))
    blk = st.enter_context(nc.Block())
    kb.finalize(blk, sems)
    st.close()
    return nc



def _host_stage(x_full, w_atten, w_proj, w_gate, w_up, w_down):
    wq = w_atten[0:C, :]
    wk = w_atten[C:2 * C, :]
    wv = w_atten[2 * C:3 * C, :]
    wqT = np.ascontiguousarray(wq.T)
    wkT = np.ascontiguousarray(wk.T)
    wvT = np.ascontiguousarray(wv.T)
    wpT = np.ascontiguousarray(w_proj.T)
    wgu = np.empty((C, 2 * I), dtype=np.float32)
    for ip in range(NIP):
        wgu[:, ip * 512:ip * 512 + 256] = w_gate[:, ip * 256:(ip + 1) * 256]
        wgu[:, ip * 512 + 256:(ip + 1) * 512] = w_up[:, ip * 256:(ip + 1) * 256]
    pm = np.zeros((128, 128), dtype=ml_dtypes.bfloat16)
    for i in range(64):
        pm[2 * i + 1, 2 * i] = -1.0
        pm[2 * i, 2 * i + 1] = 1.0
    ident = np.eye(128, dtype=ml_dtypes.bfloat16)
    theta = 1.0 / 10000.0 ** (np.arange(0, D, 2, dtype=np.float64) / D)

    in_maps = []
    for core in range(8):
        b, p = core // 2, core % 2
        blocks = [2 * j + p for j in range(8)] + [2 * j + 1 - p for j in range(8)]
        perm = np.concatenate([np.arange(128 * g, 128 * g + 128) for g in blocks])
        xTc = np.ascontiguousarray(x_full[b].T[:, perm])
        ang = np.outer(theta, perm.astype(np.float64))
        ropeCc = np.repeat(np.cos(ang), 2, axis=0).astype(np.float32)
        ropeSc = np.repeat(np.sin(ang), 2, axis=0).astype(np.float32)
        mk = np.zeros((128, 8 * 512), dtype=np.float32)
        for rel in range(4):
            m = np.zeros((128, 512), dtype=np.float32)
            for u in range(4):
                if rel > u:
                    m[:, u * 128:(u + 1) * 128] = -1e30
                elif rel == u:
                    blkm = np.where(np.tril(np.ones((128, 128), dtype=bool), -1),
                                    np.float32(-1e30), np.float32(0.0))
                    m[:, u * 128:(u + 1) * 128] = blkm
            mk[:, rel * 512:(rel + 1) * 512] = m
        for rel in range(4):
            m = np.zeros((128, 512), dtype=np.float32)
            for u in range(4):
                valid = (rel < u) if p == 0 else (rel <= u)
                if not valid:
                    m[:, u * 128:(u + 1) * 128] = -1e30
            mk[:, (4 + rel) * 512:(5 + rel) * 512] = m
        in_maps.append({
            "xT": xTc,
            "wqT": wqT, "wkT": wkT, "wvT": wvT, "wpT": wpT,
            "wgu": wgu, "wd": np.ascontiguousarray(w_down),
            "ropeC": ropeCc, "ropeS": ropeSc,
            "pmat": pm, "ident": ident,
            "masks": mk.astype(ml_dtypes.bfloat16),
        })
    return in_maps


def kernel(L0_attn_x, L0_attn_w_atten, L0_attn_w_proj,
           L0_ffn_w_gate, L0_ffn_w_up, L0_ffn_w_down):
    x = np.asarray(L0_attn_x, dtype=np.float32)
    in_maps = _host_stage(x, np.asarray(L0_attn_w_atten, np.float32),
                          np.asarray(L0_attn_w_proj, np.float32),
                          np.asarray(L0_ffn_w_gate, np.float32),
                          np.asarray(L0_ffn_w_up, np.float32),
                          np.asarray(L0_ffn_w_down, np.float32))
    nc = build_nc()
    res = run_bass_kernel_spmd(nc, in_maps, core_ids=list(range(8)))
    outf = np.empty((B, T, C), dtype=np.float32)
    for core in range(8):
        b, p = core // 2, core % 2
        oc = np.asarray(res.results[core]["out"])
        for j in range(8):
            gb = 2 * j + p
            outf[b, gb * 128:(gb + 1) * 128, :] = oc[:, j * 128:(j + 1) * 128].T
    return outf
